# version 01
# baseline (speedup 1.0000x reference)
import numpy as np

import concourse.bass as bass
import concourse.mybir as mybir
from concourse.bass_utils import run_bass_kernel_spmd
from concourse.library_overlay import lower_extended_insts
from concourse import dve_ops
from concourse.dve_op import DveOp
from concourse.dve_spec import (
    Spec, Src0, Src1, C0, C1, C2, Zero, maxx, minn, lower as dve_lower,
)

F32 = mybir.dt.float32
BF16 = mybir.dt.bfloat16
ALU = mybir.AluOpType
AFT = mybir.ActivationFunctionType
AX = mybir.AxisListType

MAGIC = 12582912.0


def _f(in0):
    return np.float64 if in0.dtype == np.float64 else np.float32


def _ref_mulb(in0, in1, s0, s1, imm2):
    return (in0.astype(_f(in0)) * in1).astype(_f(in0))


def _ref_v2(in0, in1, s0, s1, imm2):
    p = in0.astype(_f(in0)) * in1
    return (p + p).astype(_f(in0))


def _rne(v, f):
    if f is np.float64:
        return np.round(v)
    m = np.float32(MAGIC)
    return ((v.astype(np.float32) + m) - m).astype(np.float32)


def _ref_a(in0, in1, s0, s1, imm2):
    f = _f(in0)
    v = in0.astype(f)
    return np.clip(_rne(v, f), -s1, s1).astype(f)


def _ref_b1(in0, in1, s0, s1, imm2):
    f = _f(in0)
    v = in0.astype(f)
    return ((v > s0).astype(f) - (v < -s0).astype(f)
            + (v > s1).astype(f) - (v < -s1).astype(f)).astype(f)


def _ref_b2(in0, in1, s0, s1, imm2):
    f = _f(in0)
    v = in0.astype(f)
    d = (v > s0).astype(f) - (v < -s0).astype(f)
    return (d + d).astype(f)


def _register_ops():
    specs = {
        "ANT_MULB": (Src0 * Src1, _ref_mulb, 27),
        "ANT_V2": ((lambda p: p + p)(Src0 * Src1), _ref_v2, 28),
        "ANT_A": (minn(maxx((Src0 + C0) - C0, Zero - C1), C1), _ref_a, 29),
        "ANT_B1": (((Src0 > C0) - (Src0 < (Zero - C0)))
                   + ((Src0 > C1) - (Src0 < (Zero - C1))), _ref_b1, 30),
        "ANT_B2": ((lambda d: d + d)((Src0 > C0) - (Src0 < (Zero - C0))),
                   _ref_b2, 31),
    }
    out = {}
    have = {o.name: o for o in dve_ops.OPS}
    for name, (body, ref, row) in specs.items():
        if name in have:
            out[name] = have[name]
            continue
        spec = Spec(body=body)
        compiled = dve_lower(spec, ver="v3", perf_en=False, name=name)
        op = DveOp(name, spec, subdim=False,
                   uops_sha={"v3": compiled.sha("v3")}, reference=ref)
        dve_ops.OPS.append(op)
        dve_ops._SUB_OPCODE_FOR_NAME[name] = row
        dve_ops.CUSTOM_DVE_SPECS[name] = spec
        dve_ops.CUSTOM_DVE_REFS[name] = ref
        out[name] = op
    return out


OPS = _register_ops()


ENGINES = ("sync", "gpsimd", "vector", "scalar", "tensor")


class Plan:
    def __init__(self, nc):
        self.nc = nc
        self.q = {e: [] for e in ENGINES}
        self.sem = {}
        self.n = {}

    def mksem(self, ctx, *names):
        for nm in names:
            self.sem[nm] = ctx.enter_context(self.nc.semaphore(nm))
            self.n[nm] = 0

    def emit(self, eng, make, waits=(), inc=None, k=1):
        w = tuple(waits)
        if inc is not None:
            self.n[inc] += k

        def fn(e, make=make, w=w, inc=inc, k=k):
            for s, v in w:
                if v > 0:
                    e.wait_ge(self.sem[s], v)
            r = make(e)
            if inc is not None:
                if r is None:
                    r = e.nop()
                r.then_inc(self.sem[inc], k)
        self.q[eng].append(fn)
        return self.n[inc] if inc is not None else None

    def barrier(self, tag):
        tgt = self.n["ph"] + len(ENGINES)
        fence = self.fence_makers
        self.emit("sync", fence["sync"], waits=(("ld", self.n["ld"]),),
                  inc="ph")
        self.emit("gpsimd", fence["gpsimd"], waits=(("wr", self.n["wr"]),),
                  inc="ph")
        for e in ("vector", "scalar", "tensor"):
            self.emit(e, fence[e], inc="ph")
        for e in ENGINES:
            self.emit(e, lambda en: None, waits=(("ph", tgt),))

    def run(self, block):
        for eng in ENGINES:
            getattr(block, eng)(self._runner(eng))

    def _runner(self, eng):
        fns = self.q[eng]

        def go(e):
            for fn in fns:
                fn(e)
        return go



def build_program(n_tiles=16, ss=4, stop_after=None, taps=(), skip_w=False, wmode='full'):
    from contextlib import ExitStack

    T = n_tiles * 128
    E, I, HD = 1024, 2816, 64
    NE = E // 128
    NI = I // 128
    IC = [(c * 512, min(512, I - c * 512)) for c in range((I + 511) // 512)]
    NH = E // 128
    assert n_tiles % ss == 0

    nc = bass.Bass("TRN2", target_bir_lowering=False,
                   dynamic_dma_scratch_size=8192)

    x_in = nc.dram_tensor("x", [T, E], F32, kind="ExternalInput").ap()
    wq_in = nc.dram_tensor("wqT", [E, HD], F32, kind="ExternalInput").ap()
    wk_in = nc.dram_tensor("wkT", [E, HD], F32, kind="ExternalInput").ap()
    wv_in = nc.dram_tensor("wvT", [E, HD], F32, kind="ExternalInput").ap()
    wg_in = nc.dram_tensor("w_gate", [I, E], F32, kind="ExternalInput").ap()
    wu_in = nc.dram_tensor("w_up", [I, E], F32, kind="ExternalInput").ap()
    wd_in = nc.dram_tensor("w_down", [E, I], F32, kind="ExternalInput").ap()
    out = nc.dram_tensor("out", [T, E], F32, kind="ExternalOutput").ap()
    wd_scr = [nc.dram_tensor(f"wd_scr{j}", [128, NI * 128], BF16,
                             kind="ExternalOutput").ap() for j in range(NH)]

    tap_out = {}
    def mktap(name, shape, dtype):
        tap_out[name] = nc.dram_tensor("tap_" + name, shape, dtype,
                                       kind="ExternalOutput").ap()
        return tap_out[name]

    ctx = ExitStack()
    with ctx:
        p = Plan(nc)
        p.mksem(ctx, "ld", "wr", "v", "s", "pe", "ph")

        def fil(e):
            e.tensor_copy(filA[:], filB[:])

        p.fence_makers = {
            "sync": lambda e: e.nop(),
            "gpsimd": lambda e: e.engine_nop(),
            "vector": lambda e: e.tensor_copy(filA[:], filB[:]),
            "scalar": lambda e: e.activation(filA[:], filB[:], AFT.Identity),
            "tensor": lambda e: e.nop(),
        }

        def tap(name, src_ap, after_sem):
            if name not in taps:
                return
            d = mktap(name, src_ap.shape, src_ap.dtype)
            p.emit("gpsimd", lambda e: e.dma_start(out=d[:], in_=src_ap),
                   waits=(after_sem,), inc="wr", k=16)

        def sb(name, shape, dtype):
            return ctx.enter_context(nc.sbuf_tensor(name, shape, dtype))

        ident = sb("ident", [128, 128], BF16)
        x_res = sb("x_res", [128, n_tiles, E], F32)
        wgq = sb("wgq", [128, NE, I], BF16)
        wuq = sb("wuq", [128, NE, I], BF16)
        qkvw = sb("qkvw", [128, 3, NE, HD], BF16)
        svar = sb("svar", [128, n_tiles], F32)
        ssd = sb("ssd", [128, n_tiles], F32)
        rsig = sb("rsig", [128, n_tiles], F32)
        nbias = sb("nbias", [128, n_tiles], F32)
        smean = sb("smean", [128, n_tiles], F32)

        bnS = sb("bnS", [128, 12], F32)
        filA = sb("filA", [128, 1], F32)
        filB = sb("filB", [128, 1], F32)
        mv1 = sb("mv1", [128, 2], F32)
        mv2 = sb("mv2", [128, 2], F32)
        sm1 = sb("sm1", [128, n_tiles], F32)
        sv1 = sb("sv1", [128, n_tiles], F32)
        sm2 = sb("sm2", [128, n_tiles], F32)
        sv2 = sb("sv2", [128, n_tiles], F32)
        tmpa = sb("tmpa", [128, n_tiles], F32)
        tmpb = sb("tmpb", [128, n_tiles], F32)
        tmpc = sb("tmpc", [128, n_tiles], F32)
        tmpd = sb("tmpd", [128, n_tiles], F32)

        qv = sb("qv", [128, I], F32)
        qa = sb("qa", [128, I], BF16)
        qb1 = sb("qb1", [128, I], BF16)
        qb2 = sb("qb2", [128, I], BF16)
        bmax = sb("bmax", [128, I // 16], F32)
        scl = sb("scl", [128, I // 16], F32)
        rs2 = sb("rs2", [128, I // 16], F32)
        shl = sb("shl", [128, I // 16], F32)

        wtmp_holder = []
        def mk_ident(e):
            e.memset(ident[:], 0.0)
            return e.affine_select(out=ident[:], in_=ident[:],
                                   compare_op=ALU.not_equal, fill=1.0, base=0,
                                   pattern=[[-1, 128]], channel_multiplier=1)
        p.emit("gpsimd", mk_ident, inc="s")
        ident_ready = p.n["s"]

        p.emit("sync", lambda e: e.dma_start(
            out=x_res[:], in_=x_in[:].rearrange("(t p) e -> p t e", p=128)),
            inc="ld", k=16)
        x_loaded = p.n["ld"]
        tap("x0", x_res[:, 0], ("ld", x_loaded))
        p.tap_fn = tap

        def quant(width, src_ap, waits=(), sh_mult=0.5):
            nb = width // 16
            p.emit("vector", lambda e: (fil(e), e.tensor_reduce(
                bmax[:, :nb], src_ap.rearrange("p (b k) -> p b k", k=16),
                AX.X, ALU.max, apply_absolute_value=True))[1], waits=waits)
            p.emit("vector", lambda e: (fil(e), e.tensor_scalar(
                scl[:, :nb], bmax[:, :nb], 1.0 / 6.0, 1e-12,
                ALU.mult, ALU.max))[1])
            p.emit("vector", lambda e: (fil(e), e.reciprocal(
                rs2[:, :nb], scl[:, :nb]))[1])
            p.emit("vector", lambda e: e.tensor_scalar_mul(
                shl[:, :nb], scl[:, :nb], sh_mult))
            rsb = rs2[:, :nb].unsqueeze(2).broadcast_to((128, nb, 16))
            shb = shl[:, :nb].unsqueeze(2).broadcast_to((128, nb, 16))
            v3 = qv[:, :width].rearrange("p (b k) -> p b k", k=16)
            s3 = src_ap.rearrange("p (b k) -> p b k", k=16)
            p.emit("vector", lambda e: e._custom_dve(
                OPS["ANT_V2"], out=v3, in0=s3, in1=rsb))
            p.emit("vector", lambda e: (fil(e), e._custom_dve(
                OPS["ANT_A"], out=qa[:, :width], in0=qv[:, :width],
                s0=MAGIC, s1=4.0))[1])
            p.emit("vector", lambda e: e._custom_dve(
                OPS["ANT_B1"], out=qb1[:, :width], in0=qv[:, :width],
                s0=5.0, s1=7.0))
            p.emit("vector", lambda e: e._custom_dve(
                OPS["ANT_B2"], out=qb2[:, :width], in0=qv[:, :width],
                s0=10.0))
            p.emit("vector", lambda e: (fil(e), e.tensor_tensor(
                qb1[:, :width], qb1[:, :width], qb2[:, :width], ALU.add))[1])
            p.emit("vector", lambda e: (fil(e), e.scalar_tensor_tensor(
                qa[:, :width], qb1[:, :width], 2.0, qa[:, :width],
                ALU.mult, ALU.add))[1])
            q3 = qb2[:, :width].rearrange("p (b k) -> p b k", k=16)
            a3 = qa[:, :width].rearrange("p (b k) -> p b k", k=16)
            return p.emit("vector", lambda e: (fil(e), e._custom_dve(
                OPS["ANT_MULB"], out=q3, in0=a3, in1=shb))[1], inc="v")

        wctx = ExitStack()
        with wctx:
            wld = wctx.enter_context(
                nc.sbuf_tensor("wld", [128, 2, 1408], F32))
            wstg = wctx.enter_context(nc.sbuf_tensor("wstg", [128, 1024], BF16))
            wtmp = wctx.enter_context(nc.sbuf_tensor("wtmp", [128, NE, HD], F32))
            ptrw = wctx.enter_context(nc.psum_tensor("ptrw", [128, 1024], BF16))

            cons_v = [0, 0]
            qkv_ready = 0
            for j, win in enumerate((wq_in, wk_in, wv_in)):
                t = p.emit("sync", lambda e, win=win: e.dma_start(
                    out=wtmp[:],
                    in_=win[:].rearrange("(c p) d -> p c d", p=128)),
                    waits=(("v", qkv_ready),), inc="ld", k=16)
                qkv_ready = p.emit(
                    "vector",
                    lambda e, j=j: e.tensor_copy(qkvw[:, j], wtmp[:]),
                    waits=(("ld", t),), inc="v")
            p.qkv_ready = qkv_ready

            def wtile(buf, src_dram_ap, width, dst_kind, dst_idx, sh_mult,
                      goff=0):
                t_ld = p.emit("sync", lambda e: e.dma_start(
                    out=wld[:, buf, :width], in_=src_dram_ap),
                    waits=(("v", cons_v[buf]),), inc="ld", k=16)
                if wmode == "dma":
                    cons_v[buf] = p.emit("vector", lambda e: e.tensor_copy(
                        filA[:], wld[:1, buf, 0:1]), waits=(("ld", t_ld),),
                        inc="v")
                    return
                t_q = quant(width, wld[:, buf, :width],
                            waits=(("ld", t_ld),), sh_mult=sh_mult)
                cons_v[buf] = t_q
                if wmode == "quant":
                    return
                if wmode == "nodown" and dst_kind == "down":
                    return
                nblk = width // 128
                for g0 in range(0, nblk, 8):
                    gn = min(8, nblk - g0)

                    def mk_tr(e, g0=g0, gn=gn):
                        r = None
                        for j in range(gn):
                            r = e.transpose(
                                ptrw[:, j * 128:(j + 1) * 128],
                                qb2[:, (g0 + j) * 128:(g0 + j + 1) * 128],
                                ident[:])
                        return r
                    t_tr = p.emit("tensor", mk_tr,
                                  waits=(("v", t_q), ("s", ident_ready),
                                         ("wr", p.n["wr"]), ("v", p.n["v"])),
                                  inc="pe")
                    if wmode == "tr":
                        continue
                    if dst_kind in ("gate", "up"):
                        dst = wgq if dst_kind == "gate" else wuq

                        def mk_ev(e, g0=g0, gn=gn, dst=dst, dst_idx=dst_idx):
                            return e.tensor_copy(
                                dst[:, g0:g0 + gn,
                                    dst_idx * 128:(dst_idx + 1) * 128],
                                ptrw[:, :gn * 128]
                                .rearrange("p (c n) -> p c n", n=128))
                        p.emit("vector", mk_ev, waits=(("pe", t_tr),), inc="v")
                    else:
                        if wmode == "downnoev":
                            continue
                        def mk_ev(e, gn=gn):
                            return e.tensor_copy(wstg[:, :gn * 128],
                                                 ptrw[:, :gn * 128])
                        t_ev = p.emit("vector", mk_ev,
                                      waits=(("pe", t_tr),), inc="v")
                        if wmode == "downnost":
                            continue

                        def mk_st(e, g0=g0 + goff, gn=gn, dst_idx=dst_idx):
                            return e.dma_start(
                                out=wd_scr[dst_idx][:,
                                           g0 * 128:(g0 + gn) * 128],
                                in_=wstg[:, :gn * 128])
                        p.emit("gpsimd", mk_st, waits=(("v", t_ev),),
                               inc="wr", k=16)

            wg_t = wg_in.rearrange("(c p) e -> c p e", p=128)
            wu_t = wu_in.rearrange("(c p) e -> c p e", p=128)
            wd_t = wd_in.rearrange("(c p) i -> c p i", p=128)
            bi = 0
            if not skip_w:
                for i in range(NI):
                    wtile(bi % 2, wg_t[i], E, "gate", i, 0.5); bi += 1
                for i in range(NI):
                    wtile(bi % 2, wu_t[i], E, "up", i, 0.5); bi += 1
                for i in range(NH):
                    for h in range(2):
                        wtile(bi % 2, wd_t[i][:, h * 1408:(h + 1) * 1408],
                              1408, "down", i, 0.25, goff=h * 11)
                        bi += 1

            p.barrier("w_done")

        def ln_stats_and_coeffs(first_wait):
            NT = n_tiles
            for t in range(n_tiles):
                def mk_bn(e, t=t):
                    e.bn_stats(bnS[:, 0:6], x_res[:, t, 0:512])
                    e.bn_stats(bnS[:, 6:12], x_res[:, t, 512:1024])
                    e.bn_aggr(mv1[:], bnS[:, 0:6])
                    e.bn_aggr(mv2[:], bnS[:, 6:12])
                    e.tensor_copy(sm1[:, t:t + 1], mv1[:, 0:1])
                    e.tensor_copy(sv1[:, t:t + 1], mv1[:, 1:2])
                    e.tensor_copy(sm2[:, t:t + 1], mv2[:, 0:1])
                    return e.tensor_copy(sv2[:, t:t + 1], mv2[:, 1:2])
                p.emit("vector", mk_bn,
                       waits=(first_wait,) if t == 0 else ())
            t_bn = p.emit("vector", lambda e: e.tensor_copy(
                filA[:], filB[:]), inc="v")
            tap("sm1", sm1[:, :n_tiles], ("v", t_bn))
            tap("sv1", sv1[:, :n_tiles], ("v", t_bn))
            tap("sm2", sm2[:, :n_tiles], ("v", t_bn))
            tap("sv2", sv2[:, :n_tiles], ("v", t_bn))

            def mk_comb(e):
                NTT = n_tiles
                e.tensor_tensor(smean[:, :NTT], sm1[:, :NTT], sm2[:, :NTT],
                                ALU.add)
                e.tensor_tensor(tmpa[:, :NTT], sm1[:, :NTT], sm1[:, :NTT],
                                ALU.mult)
                e.tensor_tensor(tmpb[:, :NTT], sm2[:, :NTT], sm2[:, :NTT],
                                ALU.mult)
                e.tensor_tensor(tmpc[:, :NTT], sv1[:, :NTT], sv2[:, :NTT],
                                ALU.add)
                e.tensor_scalar_mul(smean[:, :NTT], smean[:, :NTT], 0.5)
                e.tensor_tensor(tmpa[:, :NTT], tmpa[:, :NTT], tmpb[:, :NTT],
                                ALU.add)
                e.tensor_tensor(tmpd[:, :NTT], smean[:, :NTT], smean[:, :NTT],
                                ALU.mult)
                e.tensor_tensor(tmpa[:, :NTT], tmpa[:, :NTT], tmpc[:, :NTT],
                                ALU.add)
                e.tensor_scalar_add(tmpd[:, :NTT], tmpd[:, :NTT], -1e-5)
                fil(e)
                return e.scalar_tensor_tensor(
                    svar[:, :NTT], tmpa[:, :NTT], 0.5, tmpd[:, :NTT],
                    ALU.mult, ALU.subtract)
            t_var = p.emit("vector", mk_comb, inc="v")
            t_sd = p.emit("scalar", lambda e: e.activation(
                ssd[:, :NT], svar[:, :NT], AFT.Sqrt),
                waits=(("v", t_var),), inc="s")
            p.emit("vector", lambda e: e.reciprocal(
                rsig[:, :NT], ssd[:, :NT]), waits=(("s", t_sd),))
            t_ret = p.emit("vector", lambda e: (fil(e),
                e.scalar_tensor_tensor(
                    nbias[:, :NT], smean[:, :NT], -1.0, rsig[:, :NT],
                    ALU.mult, ALU.mult))[1], inc="v")
            tap("rsig", rsig[:, :NT], ("v", t_ret))
            tap("x0b", x_res[:, 0], ("v", t_ret))
            tap("nbias", nbias[:, :NT], ("v", t_ret))
            return t_ret

        def ffn_phase(tag):
            fctx = ExitStack()
            with fctx:
                xqT = fctx.enter_context(
                    nc.sbuf_tensor(f"xqT{tag}", [128, NE, 128], BF16))
                hqT = fctx.enter_context(
                    nc.sbuf_tensor(f"hqT{tag}", [128, NI, ss * 128], BF16))
                sg = fctx.enter_context(
                    nc.sbuf_tensor(f"sg{tag}", [128, 512], F32))
                dnb = fctx.enter_context(
                    nc.sbuf_tensor(f"dnb{tag}", [128, ss * 128], BF16))
                wdq = fctx.enter_context(
                    nc.sbuf_tensor(f"wdq{tag}", [128, 2, NI, 128], BF16))
                pg = fctx.enter_context(
                    nc.psum_tensor(f"pg{tag}", [128, 512], F32))
                pu = fctx.enter_context(
                    nc.psum_tensor(f"pu{tag}", [128, 512], F32))
                ptr = fctx.enter_context(
                    nc.psum_tensor(f"ptf{tag}", [128, 1024], BF16))
                pdn = fctx.enter_context(
                    nc.psum_tensor(f"pdn{tag}", [128, ss * 128], F32))

                t_ln = ln_stats_and_coeffs(("ld", x_loaded))

                sts = [] if stop_after == "stats" else list(range(n_tiles // ss))
                wd_cons = [0, 0]
                for st in sts:
                    tiles = list(range(st * ss, (st + 1) * ss))
                    for t in tiles:
                        t_xln = p.emit("scalar", lambda e, t=t: e.activation(
                            qv[:, :E], x_res[:, t], AFT.Identity,
                            bias=nbias[:, t:t + 1], scale=rsig[:, t:t + 1]),
                            waits=(("v", t_ln), ("v", p.n["v"])), inc="s")
                        if t == 0:
                            tap(f"xln{tag}", qv[:, :E], ("s", t_xln))
                        t_q = quant(E, qv[:, :E], waits=(("s", t_xln),))
                        if t == 0:
                            tap(f"xq{tag}", qb2[:, :E], ("v", t_q))

                        def mk_trx(e):
                            r = None
                            for j in range(NE):
                                r = e.transpose(ptr[:, j * 128:(j + 1) * 128],
                                                qb2[:, j * 128:(j + 1) * 128],
                                                ident[:])
                            return r
                        t_trx = p.emit("tensor", mk_trx,
                                       waits=(("v", t_q), ("v", p.n["v"])),
                                       inc="pe")
                        t_evx = p.emit("vector", lambda e: e.tensor_copy(
                            xqT[:],
                            ptr[:].rearrange("p (c n) -> p c n", n=128)),
                            waits=(("pe", t_trx),), inc="v")

                        for c0, cw in IC:
                            def mk_mm(e, c0=c0, cw=cw):
                                r = None
                                for j in range(NE):
                                    e.matmul(pg[:, :cw], xqT[:, j],
                                             wgq[:, j, c0:c0 + cw],
                                             start=(j == 0),
                                             stop=(j == NE - 1))
                                    r = e.matmul(pu[:, :cw], xqT[:, j],
                                                 wuq[:, j, c0:c0 + cw],
                                                 start=(j == 0),
                                                 stop=(j == NE - 1))
                                return r
                            t_mm = p.emit("tensor", mk_mm,
                                          waits=(("v", t_evx), ("v", p.n["v"]),
                                                 ("s", p.n["s"])), inc="pe")
                            t_silu = p.emit(
                                "scalar", lambda e, cw=cw: e.activation(
                                    sg[:, :cw], pg[:, :cw], AFT.Silu),
                                waits=(("pe", t_mm), ("v", p.n["v"])),
                                inc="s")
                            p.emit("vector",
                                   lambda e, c0=c0, cw=cw: e.tensor_tensor(
                                       qv[:, c0:c0 + cw], sg[:, :cw],
                                       pu[:, :cw], ALU.mult),
                                   waits=(("s", t_silu), ("pe", t_mm)),
                                   inc="v")
                        if t == 0:
                            tap(f"h{tag}", qv[:, :I], ("v", p.n["v"]))
                        t_qh = quant(I, qv[:, :I])
                        if t == 0:
                            tap(f"hq{tag}", qb2[:, :I], ("v", t_qh))
                        si = t - st * ss
                        for g0 in range(0, NI, 8):
                            gn = min(8, NI - g0)

                            def mk_trh(e, g0=g0, gn=gn):
                                r = None
                                for j in range(gn):
                                    r = e.transpose(
                                        ptr[:, j * 128:(j + 1) * 128],
                                        qb2[:, (g0 + j) * 128:
                                            (g0 + j + 1) * 128],
                                        ident[:])
                                return r
                            t_trh = p.emit("tensor", mk_trh,
                                           waits=(("v", t_qh),
                                                  ("v", p.n["v"])), inc="pe")

                            def mk_evh(e, g0=g0, gn=gn, si=si):
                                return e.tensor_copy(
                                    hqT[:, g0:g0 + gn,
                                        si * 128:(si + 1) * 128],
                                    ptr[:, :gn * 128]
                                    .rearrange("p (c n) -> p c n", n=128))
                            p.emit("vector", mk_evh,
                                   waits=(("pe", t_trh),), inc="v")
                    t_hq_done = p.n["v"]

                    for hs in range(NH):
                        buf = hs % 2
                        t_wd = p.emit(
                            "sync", lambda e, hs=hs, buf=buf: e.dma_start(
                                out=wdq[:, buf].rearrange("p c n -> p (c n)"),
                                in_=wd_scr[hs][:]),
                            waits=(("pe", wd_cons[buf]), ("wr", p.n["wr"])),
                            inc="ld", k=16)

                        def mk_dn(e, buf=buf):
                            r = None
                            for j in range(NI):
                                r = e.matmul(pdn[:], wdq[:, buf, j],
                                             hqT[:, j],
                                             start=(j == 0),
                                             stop=(j == NI - 1))
                            return r
                        t_dn = p.emit("tensor", mk_dn,
                                      waits=(("ld", t_wd), ("v", t_hq_done),
                                             ("v", p.n["v"]),
                                             ("s", p.n["s"])), inc="pe")
                        wd_cons[buf] = t_dn
                        t_evd = p.emit("scalar", lambda e: e.activation(
                            dnb[:], pdn[:], AFT.Identity),
                            waits=(("pe", t_dn),), inc="s")

                        def mk_trb(e):
                            r = None
                            for si in range(ss):
                                r = e.transpose(
                                    ptr[:, si * 128:(si + 1) * 128],
                                    dnb[:, si * 128:(si + 1) * 128],
                                    ident[:])
                            return r
                        t_trb = p.emit("tensor", mk_trb,
                                       waits=(("s", t_evd), ("v", p.n["v"])),
                                       inc="pe")

                        def mk_add(e, hs=hs, st=st):
                            r = None
                            for si in range(ss):
                                t = st * ss + si
                                r = e.tensor_tensor(
                                    x_res[:, t, hs * 128:(hs + 1) * 128],
                                    x_res[:, t, hs * 128:(hs + 1) * 128],
                                    ptr[:, si * 128:(si + 1) * 128], ALU.add)
                            return r
                        p.emit("vector", mk_add, waits=(("pe", t_trb),),
                               inc="v")

            p.barrier(f"ffn{tag}_done")

        def attn_phase():
            actx = ExitStack()
            with actx:
                xlT = actx.enter_context(
                    nc.sbuf_tensor("xlT", [128, 2, NE, 128], BF16))
                dump = actx.enter_context(
                    nc.sbuf_tensor("dump", [128, E], BF16))
                qT = actx.enter_context(nc.sbuf_tensor("qT", [64, T], BF16))
                kT = actx.enter_context(nc.sbuf_tensor("kT", [64, T], BF16))
                vv = actx.enter_context(
                    nc.sbuf_tensor("vv", [128, n_tiles, HD], BF16))
                vsc = actx.enter_context(
                    nc.sbuf_tensor("vsc", [128, HD], BF16))
                attnT = actx.enter_context(
                    nc.sbuf_tensor("attnT", [128, T], BF16))
                mx4 = actx.enter_context(nc.sbuf_tensor("mx4", [128, 4], F32))
                rmax = actx.enter_context(
                    nc.sbuf_tensor("rmax", [128, 1], F32))
                zrow = actx.enter_context(
                    nc.sbuf_tensor("zrow", [128, 1], F32))
                rz = actx.enter_context(nc.sbuf_tensor("rz", [128, 1], F32))

                t_ln = ln_stats_and_coeffs(("v", p.n["v"]))
                chunks = [(c, min(512, T - c * 512))
                          for c in range((T + 511) // 512)]
                nic = len(chunks)

                sctx = ExitStack()
                with sctx:
                    ptra = sctx.enter_context(
                        nc.psum_tensor("ptra", [128, 1024], BF16))
                    pj = sctx.enter_context(
                        nc.psum_tensor("pj", [128, 512], F32))
                    pv = sctx.enter_context(
                        nc.psum_tensor("pv", [128, 512], F32))
                    xl_cons = [0, 0]
                    for t in range(n_tiles):
                        xbuf = t % 2
                        t_x = p.emit("scalar", lambda e, t=t: e.activation(
                            dump[:], x_res[:, t], AFT.Identity,
                            bias=nbias[:, t:t + 1], scale=rsig[:, t:t + 1]),
                            waits=(("v", t_ln), ("pe", p.n["pe"])), inc="s")

                        def mk_tr(e):
                            r = None
                            for j in range(NE):
                                r = e.transpose(
                                    ptra[:, j * 128:(j + 1) * 128],
                                    dump[:, j * 128:(j + 1) * 128], ident[:])
                            return r
                        t_tr = p.emit("tensor", mk_tr,
                                      waits=(("s", t_x), ("v", p.n["v"])),
                                      inc="pe")
                        t_xl = p.emit("vector",
                                      lambda e, xbuf=xbuf: e.tensor_copy(
                                          xlT[:, xbuf],
                                          ptra[:].rearrange(
                                              "p (c n) -> p c n", n=128)),
                                      waits=(("pe", t_tr),
                                             ("pe", xl_cons[xbuf])), inc="v")

                        for j2, qkdst in ((0, qT), (1, kT)):
                            def mk_qk(e, j2=j2, xbuf=xbuf):
                                r = None
                                for j in range(NE):
                                    r = e.matmul(
                                        pj[0:HD, 0:128], qkvw[:, j2, j],
                                        xlT[:, xbuf, j],
                                        start=(j == 0), stop=(j == NE - 1))
                                return r
                            t_mm = p.emit("tensor", mk_qk,
                                          waits=(("v", t_xl),
                                                 ("v", p.qkv_ready),
                                                 ("v", p.n["v"])), inc="pe")
                            p.emit("vector",
                                   lambda e, qkdst=qkdst, t=t:
                                   e.tensor_copy(
                                       qkdst[:, t * 128:(t + 1) * 128],
                                       pj[0:HD, 0:128]),
                                   waits=(("pe", t_mm),), inc="v")
                        sl = t % 8

                        def mk_v(e, sl=sl, xbuf=xbuf):
                            r = None
                            for j in range(NE):
                                r = e.matmul(pv[:, sl * 64:(sl + 1) * 64],
                                             xlT[:, xbuf, j],
                                             qkvw[:, 2, j],
                                             start=(j == 0),
                                             stop=(j == NE - 1))
                            return r
                        t_mm = p.emit("tensor", mk_v,
                                      waits=(("v", t_xl), ("v", p.n["v"])),
                                      inc="pe")
                        xl_cons[xbuf] = t_mm
                        p.emit("vector", lambda e, t=t, sl=sl: e.tensor_copy(
                            vv[:, t], pv[:, sl * 64:(sl + 1) * 64]),
                            waits=(("pe", t_mm),), inc="v")
                    p.barrier("attn_stageA")

                sctx2 = ExitStack()
                with sctx2:
                    ps = sctx2.enter_context(
                        nc.psum_tensor("ps", [128, 4, 512], F32))
                    po = sctx2.enter_context(
                        nc.psum_tensor("po", [128, n_tiles, HD], F32))
                    oacc = sctx2.enter_context(
                        nc.sbuf_tensor("oacc", [128, n_tiles, HD], F32))
                    t_oz = p.emit("vector", lambda e: e.memset(oacc[:], 0.0),
                                  inc="v")
                    for kt in range(n_tiles):
                        def mk_sc(e, kt=kt):
                            r = None
                            for c, cw in chunks:
                                r = e.matmul(
                                    ps[:, c, :cw],
                                    kT[:, kt * 128:(kt + 1) * 128],
                                    qT[:, c * 512:c * 512 + cw],
                                    start=True, stop=True)
                            return r
                        t_sc = p.emit("tensor", mk_sc,
                                      waits=(("v", p.n["v"]),
                                             ("s", p.n["s"])), inc="pe")

                        def mk_mx(e):
                            r = None
                            for c, cw in chunks:
                                r = e.tensor_reduce(mx4[:, c:c + 1],
                                                    ps[:, c, :cw],
                                                    AX.X, ALU.max)
                            fil(e)
                            return r
                        p.emit("vector", mk_mx,
                               waits=(("pe", t_sc), ("s", p.n["s"])))
                        t_nm = p.emit("vector", lambda e: e.tensor_reduce(
                            rmax[:], mx4[:, :nic], AX.X, ALU.max,
                            negate=True), inc="v")
                        t_exp = 0
                        pe_prev = p.n["pe"]
                        for c, cw in chunks:
                            t_exp = p.emit(
                                "scalar", lambda e, c=c, cw=cw: e.activation(
                                    attnT[:, c * 512:c * 512 + cw],
                                    ps[:, c, :cw], AFT.Exp, bias=rmax[:]),
                                waits=(("v", t_nm), ("pe", pe_prev)),
                                inc="s")
                        t_z = p.emit("vector", lambda e: e.tensor_reduce(
                            zrow[:], attnT[:], AX.X, ALU.add),
                            waits=(("s", t_exp),), inc="v")
                        p.emit("vector",
                               lambda e: (fil(e),
                                          e.reciprocal(rz[:], zrow[:]))[1])
                        t_vs = p.emit("vector",
                                      lambda e, kt=kt: (fil(e),
                                      e.tensor_scalar_mul(
                                          vsc[:], vv[:, kt], rz[:]))[1],
                                      waits=(("pe", p.n["pe"]),), inc="v")

                        def mk_out(e):
                            r = None
                            for it in range(n_tiles):
                                r = e.matmul(
                                    po[:, it],
                                    attnT[:, it * 128:(it + 1) * 128],
                                    vsc[:], start=True, stop=True)
                            return r
                        t_om = p.emit("tensor", mk_out,
                                      waits=(("v", t_vs), ("v", t_oz),
                                             ("v", p.n["v"])), inc="pe")

                        def mk_oadd(e):
                            r = None
                            for it in range(n_tiles):
                                r = e.tensor_tensor(oacc[:, it], oacc[:, it],
                                                    po[:, it], ALU.add)
                            return r
                        p.emit("vector", mk_oadd, waits=(("pe", t_om),),
                               inc="v")
                    t_oall = p.n["v"]
                    for it in range(n_tiles):
                        def mk_radd(e, it=it):
                            r = None
                            for rr in range(16):
                                r = e.tensor_tensor(
                                    x_res[:, it, rr * 64:(rr + 1) * 64],
                                    x_res[:, it, rr * 64:(rr + 1) * 64],
                                    oacc[:, it], ALU.add)
                            return r
                        p.emit("vector", mk_radd, waits=(("v", t_oall),),
                               inc="v")
                    p.barrier("attn_done")

        if stop_after != "w":
            ffn_phase(1)
            if stop_after not in ("ffn1", "stats"):
                attn_phase()
                if stop_after != "attn":
                    ffn_phase(3)

        t_all = p.n["v"]
        p.emit("gpsimd", lambda e: e.dma_start(
            out=out[:].rearrange("(t p) e -> p t e", p=128), in_=x_res[:]),
            waits=(("v", t_all),), inc="wr", k=16)
        fin = p.n["wr"]
        p.emit("gpsimd", lambda e: None, waits=(("wr", fin),))

        with nc.Block() as block:
            p.run(block)

    return nc



_CACHE = {}


def _get_nc(n_tiles=16, ss=2):
    key = (n_tiles, ss)
    if key not in _CACHE:
        nc = build_program(n_tiles=n_tiles, ss=ss)
        lower_extended_insts(nc)
        _CACHE[key] = nc
    return _CACHE[key]


def kernel(L0_attn_x, L0_attn_Wq, L0_attn_Wk, L0_attn_Wv,
           L0_ffn_w_gate, L0_ffn_w_up, L0_ffn_w_down):
    x = np.ascontiguousarray(np.asarray(L0_attn_x, dtype=np.float32))
    wqT = np.ascontiguousarray(np.asarray(L0_attn_Wq, dtype=np.float32).T)
    wkT = np.ascontiguousarray(np.asarray(L0_attn_Wk, dtype=np.float32).T)
    wvT = np.ascontiguousarray(np.asarray(L0_attn_Wv, dtype=np.float32).T)
    wg = np.ascontiguousarray(np.asarray(L0_ffn_w_gate, dtype=np.float32))
    wu = np.ascontiguousarray(np.asarray(L0_ffn_w_up, dtype=np.float32))
    wd = np.ascontiguousarray(np.asarray(L0_ffn_w_down, dtype=np.float32))
    B = x.shape[0]
    nc = _get_nc()
    in_maps = [{"x": x[b], "wqT": wqT, "wkT": wkT, "wvT": wvT,
                "w_gate": wg, "w_up": wu, "w_down": wd} for b in range(B)]
    res = run_bass_kernel_spmd(nc, in_maps, core_ids=list(range(B)))
    return np.stack([np.asarray(res.results[b]["out"]) for b in range(B)],
                    axis=0).astype(np.float32)


# version 04
# speedup vs baseline: 1.1528x; 1.1528x over previous
import numpy as np

import concourse.bass as bass
import concourse.mybir as mybir
from concourse.bass_utils import run_bass_kernel_spmd
from concourse.library_overlay import lower_extended_insts
from concourse import dve_ops
from concourse.dve_op import DveOp
from concourse.dve_spec import (
    Spec, Src0, Src1, C0, C1, C2, Zero, One, maxx, minn, lower as dve_lower,
)

F32 = mybir.dt.float32
BF16 = mybir.dt.bfloat16
ALU = mybir.AluOpType
AFT = mybir.ActivationFunctionType
AX = mybir.AxisListType

MAGIC = 12582912.0


def _f(in0):
    return np.float64 if in0.dtype == np.float64 else np.float32


def _ref_v2(in0, in1, s0, s1, imm2):
    p = in0.astype(_f(in0)) * in1
    return (p + p).astype(_f(in0))


def _rne(v, f):
    if f is np.float64:
        return np.round(v)
    m = np.float32(MAGIC)
    return ((v.astype(np.float32) + m) - m).astype(np.float32)


def _ref_qs(in0, in1, s0, s1, imm2):
    f = _f(in0)
    u = np.abs(in0.astype(f))
    return ((u > s0).astype(f) + (u > s1).astype(f)
            + (u > imm2).astype(f) + (u > imm2).astype(f)).astype(f)


def _ref_qt(in0, in1, s0, s1, imm2):
    f = _f(in0)
    v = in0.astype(f)
    a = np.clip(_rne(v, f), -s1, s1)
    return (a * (in1.astype(f) + 2.0)).astype(f)


def _qs_body():
    v = Src0
    u = maxx(v, Zero - v)
    u5 = u > C0
    u7 = u > C1
    u10 = u > C2
    return ((u5 + u7) + u10) + u10


def _qt_body():
    v = Src0
    a = minn(maxx((v + C0) - C0, Zero - C1), C1)
    return a * (Src1 + (One + One))


def _register_ops():
    specs = {
        "ANT_V2": ((lambda p: p + p)(Src0 * Src1), _ref_v2, 28),
        "ANT_QS": (_qs_body(), _ref_qs, 29),
        "ANT_QT": (_qt_body(), _ref_qt, 30),
    }
    out = {}
    have = {o.name: o for o in dve_ops.OPS}
    for name, (body, ref, row) in specs.items():
        if name in have:
            out[name] = have[name]
            continue
        spec = Spec(body=body)
        compiled = dve_lower(spec, ver="v3", perf_en=False, name=name)
        op = DveOp(name, spec, subdim=False,
                   uops_sha={"v3": compiled.sha("v3")}, reference=ref)
        dve_ops.OPS.append(op)
        dve_ops._SUB_OPCODE_FOR_NAME[name] = row
        dve_ops.CUSTOM_DVE_SPECS[name] = spec
        dve_ops.CUSTOM_DVE_REFS[name] = ref
        out[name] = op
    return out


OPS = _register_ops()


ENGINES = ("sync", "gpsimd", "vector", "scalar", "tensor")


class Plan:
    def __init__(self, nc):
        self.nc = nc
        self.q = {e: [] for e in ENGINES}
        self.sem = {}
        self.n = {}

    def mksem(self, ctx, *names):
        for nm in names:
            self.sem[nm] = ctx.enter_context(self.nc.semaphore(nm))
            self.n[nm] = 0

    def emit(self, eng, make, waits=(), inc=None, k=1):
        w = tuple(waits)
        if inc is not None:
            self.n[inc] += k

        def fn(e, make=make, w=w, inc=inc, k=k):
            for s, v in w:
                if v > 0:
                    e.wait_ge(self.sem[s], v)
            r = make(e)
            if inc is not None:
                if r is None:
                    r = e.nop()
                r.then_inc(self.sem[inc], k)
        self.q[eng].append(fn)
        return self.n[inc] if inc is not None else None

    def barrier(self, tag):
        tgt = self.n["ph"] + len(ENGINES)
        fence = self.fence_makers
        self.emit("sync", fence["sync"],
                  waits=tuple((s, self.n[s]) for s in ("lx", "lq", "la", "lb")),
                  inc="ph")
        self.emit("gpsimd", fence["gpsimd"], waits=(("wr", self.n["wr"]),),
                  inc="ph")
        for e in ("vector", "scalar", "tensor"):
            self.emit(e, fence[e], inc="ph")
        for e in ENGINES:
            self.emit(e, lambda en: None, waits=(("ph", tgt),))

    def run(self, block):
        for eng in ENGINES:
            getattr(block, eng)(self._runner(eng))

    def _runner(self, eng):
        fns = self.q[eng]

        def go(e):
            for fn in fns:
                fn(e)
        return go



def build_program(n_tiles=16, ss=2, stop_after=None, taps=()):
    from contextlib import ExitStack

    T = n_tiles * 128
    E, I, HD = 1024, 2816, 64
    NE = E // 128
    NI = I // 128
    IC = [(c * 512, min(512, I - c * 512)) for c in range((I + 511) // 512)]
    NH = E // 128
    assert n_tiles % ss == 0

    nc = bass.Bass("TRN2", target_bir_lowering=False,
                   dynamic_dma_scratch_size=8192,
                   detect_race_conditions=False)

    x_in = nc.dram_tensor("x", [T, E], F32, kind="ExternalInput").ap()
    wq_in = nc.dram_tensor("wqT", [E, HD], F32, kind="ExternalInput").ap()
    wk_in = nc.dram_tensor("wkT", [E, HD], F32, kind="ExternalInput").ap()
    wv_in = nc.dram_tensor("wvT", [E, HD], F32, kind="ExternalInput").ap()
    wg_in = nc.dram_tensor("w_gate", [I, E], F32, kind="ExternalInput").ap()
    wu_in = nc.dram_tensor("w_up", [I, E], F32, kind="ExternalInput").ap()
    wd_in = nc.dram_tensor("w_down", [E, I], F32, kind="ExternalInput").ap()
    out = nc.dram_tensor("out", [T, E], F32, kind="ExternalOutput").ap()
    wd_scr = [nc.dram_tensor(f"wd_scr{j}", [128, NI * 128], BF16,
                             kind="ExternalOutput").ap() for j in range(NH)]

    tap_out = {}
    def mktap(name, shape, dtype):
        tap_out[name] = nc.dram_tensor("tap_" + name, shape, dtype,
                                       kind="ExternalOutput").ap()
        return tap_out[name]

    ctx = ExitStack()
    with ctx:
        p = Plan(nc)
        p.mksem(ctx, "lx", "lq", "la", "lb", "wr", "v", "s", "pe", "g", "ph")

        def fil(e):
            e.tensor_copy(filA[:], filB[:])

        p.fence_makers = {
            "sync": lambda e: e.nop(),
            "gpsimd": lambda e: e.engine_nop(),
            "vector": lambda e: e.tensor_copy(filA[:], filB[:]),
            "scalar": lambda e: e.activation(filA[:], filB[:], AFT.Identity),
            "tensor": lambda e: e.nop(),
        }

        def tap(name, src_ap, after_sem):
            if name not in taps:
                return
            d = mktap(name, src_ap.shape, src_ap.dtype)
            p.emit("gpsimd", lambda e: e.dma_start(out=d[:], in_=src_ap),
                   waits=(after_sem,), inc="wr", k=16)

        def sb(name, shape, dtype):
            return ctx.enter_context(nc.sbuf_tensor(name, shape, dtype))

        ident = sb("ident", [128, 128], BF16)
        identf = sb("identf", [128, 128], F32)
        x_res = sb("x_res", [128, n_tiles, E], F32)
        wgq = sb("wgq", [128, NE, I], BF16)
        wuq = sb("wuq", [128, NE, I], BF16)
        qkvw = sb("qkvw", [128, 3, NE, HD], BF16)
        svar = sb("svar", [128, n_tiles], F32)
        ssd = sb("ssd", [128, n_tiles], F32)
        rsig = sb("rsig", [128, n_tiles], F32)
        nbias = sb("nbias", [128, n_tiles], F32)
        smean = sb("smean", [128, n_tiles], F32)

        bnS = sb("bnS", [128, 12], F32)
        filA = sb("filA", [128, 1], F32)
        filB = sb("filB", [128, 1], F32)
        mv1 = sb("mv1", [128, 2], F32)
        mv2 = sb("mv2", [128, 2], F32)
        sm1 = sb("sm1", [128, n_tiles], F32)
        sv1 = sb("sv1", [128, n_tiles], F32)
        sm2 = sb("sm2", [128, n_tiles], F32)
        sv2 = sb("sv2", [128, n_tiles], F32)
        tmpa = sb("tmpa", [128, n_tiles], F32)
        tmpb = sb("tmpb", [128, n_tiles], F32)
        tmpc = sb("tmpc", [128, n_tiles], F32)
        tmpd = sb("tmpd", [128, n_tiles], F32)

        qvh = sb("qvh", [128, I], F32)
        vbuf = sb("vbuf", [128, I], F32)
        qt_o = sb("qt_o", [128, I], BF16)
        qb2 = sb("qb2", [128, I], BF16)
        bmax = sb("bmax", [128, I // 16], F32)
        scl = sb("scl", [128, I // 16], F32)
        rs2 = sb("rs2", [128, I // 16], F32)
        shl = sb("shl", [128, 2, I // 16], F32)

        p.emit("vector", lambda e: e.memset(filB[:], 0.0))
        t_ms = p.emit("gpsimd", lambda e: e.memset(ident[:], 0.0), inc="g")
        p.emit("gpsimd", lambda e: e.affine_select(
            out=ident[:], in_=ident[:],
            compare_op=ALU.not_equal, fill=1.0, base=0,
            pattern=[[-1, 128]], channel_multiplier=1),
            waits=(("g", t_ms),), inc="s")
        t_msf = p.emit("gpsimd", lambda e: e.memset(identf[:], 0.0), inc="g")
        p.emit("gpsimd", lambda e: e.affine_select(
            out=identf[:], in_=identf[:],
            compare_op=ALU.not_equal, fill=1.0, base=0,
            pattern=[[-1, 128]], channel_multiplier=1),
            waits=(("g", t_msf),), inc="s")
        ident_ready = p.n["s"]

        p.emit("sync", lambda e: e.dma_start(
            out=x_res[:], in_=x_in[:].rearrange("(t p) e -> p t e", p=128)),
            inc="lx", k=16)
        x_loaded = p.n["lx"]
        p.tap_fn = tap

        qstate = {
            "par": 0,
            "mulb": [0, 0],
            "mulb_last": 0,
            "qb2_free": 0,
        }

        def quant(width, src_ap, waits=(), sh_mult=0.25, amax_fil=False):
            nb = width // 16
            par = qstate["par"]
            qstate["par"] ^= 1
            s3 = src_ap.rearrange("p (b k) -> p b k", k=16)
            p.emit("vector", lambda e: ((fil(e) if amax_fil else None),
                e.tensor_reduce(bmax[:, :nb], s3, AX.X, ALU.max,
                                apply_absolute_value=True))[1], waits=waits)
            p.emit("vector", lambda e: (fil(e), e.tensor_scalar(
                scl[:, :nb], bmax[:, :nb], 1.0 / 6.0, 1e-12,
                ALU.mult, ALU.max))[1])
            p.emit("vector", lambda e: (fil(e), e.reciprocal(
                rs2[:, :nb], scl[:, :nb]))[1])
            p.emit("vector", lambda e, par=par: e.tensor_scalar_mul(
                shl[:, par, :nb], scl[:, :nb], sh_mult),
                waits=(("g", qstate["mulb"][par]),))
            rsb = rs2[:, :nb].unsqueeze(2).broadcast_to((128, nb, 16))
            v3 = vbuf[:, :width].rearrange("p (b k) -> p b k", k=16)
            p.emit("vector", lambda e: e._custom_dve(
                OPS["ANT_V2"], out=v3, in0=s3, in1=rsb))
            p.emit("vector", lambda e: (fil(e), e._custom_dve(
                OPS["ANT_QS"], out=src_ap, in0=vbuf[:, :width],
                s0=5.0, s1=7.0, imm2=10.0))[1])
            t_qt = p.emit("vector", lambda e: (fil(e), e._custom_dve(
                OPS["ANT_QT"], out=qt_o[:, :width], in0=vbuf[:, :width],
                in1=src_ap, s0=MAGIC, s1=4.0))[1],
                waits=(("g", qstate["mulb_last"]),), inc="v")
            shb = shl[:, par, :nb].unsqueeze(2).broadcast_to((128, nb, 16))
            q3 = qb2[:, :width].rearrange("p (b k) -> p b k", k=16)
            t3 = qt_o[:, :width].rearrange("p (b k) -> p b k", k=16)
            t_q = p.emit("gpsimd", lambda e: e.tensor_tensor(
                q3, t3, shb, ALU.mult),
                waits=(("v", t_qt), ("pe", qstate["qb2_free"])),
                inc="g")
            qstate["mulb"][par] = t_q
            qstate["mulb_last"] = t_q
            return t_qt, t_q

        wctx = ExitStack()
        with wctx:
            wld = wctx.enter_context(
                nc.sbuf_tensor("wld", [128, 2, 1408], F32))
            wstg = wctx.enter_context(nc.sbuf_tensor("wstg", [128, 1024], BF16))
            wtmp = qvh[:, 0:NE * HD].rearrange("p (c d) -> p c d", d=HD)
            ptrw = wctx.enter_context(nc.psum_tensor("ptrw", [128, 1024], BF16))

            cons_v = [0, 0]
            qkv_ready = 0
            for j, win in enumerate((wq_in, wk_in, wv_in)):
                t = p.emit("sync", lambda e, win=win: e.dma_start(
                    out=wtmp,
                    in_=win[:].rearrange("(c p) d -> p c d", p=128)),
                    waits=(("v", qkv_ready),), inc="lq", k=16)
                qkv_ready = p.emit(
                    "vector",
                    lambda e, j=j: e.tensor_copy(qkvw[:, j], wtmp),
                    waits=(("lq", t),), inc="v")
            p.qkv_ready = qkv_ready

            def wtile(buf, src_dram_ap, width, dst_kind, dst_idx, sh_mult,
                      goff=0):
                lsem = "la" if buf == 0 else "lb"
                t_ld = p.emit("sync", lambda e: e.dma_start(
                    out=wld[:, buf, :width], in_=src_dram_ap),
                    waits=(("v", cons_v[buf]),), inc=lsem, k=16)
                t_qt, t_q = quant(width, wld[:, buf, :width],
                                  waits=((lsem, t_ld),), sh_mult=sh_mult)
                cons_v[buf] = t_qt
                nblk = width // 128
                for g0 in range(0, nblk, 8):
                    gn = min(8, nblk - g0)

                    def mk_tr(e, g0=g0, gn=gn):
                        r = None
                        for j in range(gn):
                            r = e.transpose(
                                ptrw[:, j * 128:(j + 1) * 128],
                                qb2[:, (g0 + j) * 128:(g0 + j + 1) * 128],
                                ident[:])
                        return r
                    t_tr = p.emit("tensor", mk_tr,
                                  waits=(("g", t_q), ("s", ident_ready),
                                         ("wr", p.n["wr"]), ("s", p.n["s"])),
                                  inc="pe")
                    if dst_kind in ("gate", "up"):
                        dst = wgq if dst_kind == "gate" else wuq

                        def mk_ev(e, g0=g0, gn=gn, dst=dst, dst_idx=dst_idx):
                            return e.activation(
                                dst[:, g0:g0 + gn,
                                    dst_idx * 128:(dst_idx + 1) * 128],
                                ptrw[:, :gn * 128]
                                .rearrange("p (c n) -> p c n", n=128),
                                AFT.Identity)
                        p.emit("scalar", mk_ev, waits=(("pe", t_tr),), inc="s")
                    else:
                        def mk_ev(e, gn=gn):
                            return e.activation(wstg[:, :gn * 128],
                                                ptrw[:, :gn * 128],
                                                AFT.Identity)
                        t_ev = p.emit("scalar", mk_ev,
                                      waits=(("pe", t_tr),), inc="s")

                        def mk_st(e, g0=g0 + goff, gn=gn, dst_idx=dst_idx):
                            return e.dma_start(
                                out=wd_scr[dst_idx][:,
                                           g0 * 128:(g0 + gn) * 128],
                                in_=wstg[:, :gn * 128])
                        p.emit("gpsimd", mk_st, waits=(("s", t_ev),),
                               inc="wr", k=16)
                qstate["qb2_free"] = p.n["pe"]

            wg_t = wg_in.rearrange("(c p) e -> c p e", p=128)
            wu_t = wu_in.rearrange("(c p) e -> c p e", p=128)
            wd_t = wd_in.rearrange("(c p) i -> c p i", p=128)
            bi = 0
            for i in range(NI):
                wtile(bi % 2, wg_t[i], E, "gate", i, 0.25); bi += 1
            for i in range(NI):
                wtile(bi % 2, wu_t[i], E, "up", i, 0.25); bi += 1
            for i in range(NH):
                for h in range(2):
                    wtile(bi % 2, wd_t[i][:, h * 1408:(h + 1) * 1408],
                          1408, "down", i, 0.125, goff=h * 11)
                    bi += 1

            p.barrier("w_done")

        def ln_stats_and_coeffs(first_wait):
            NT = n_tiles
            for t in range(n_tiles):
                def mk_bn(e, t=t):
                    e.bn_stats(bnS[:, 0:6], x_res[:, t, 0:512])
                    e.bn_stats(bnS[:, 6:12], x_res[:, t, 512:1024])
                    e.bn_aggr(mv1[:], bnS[:, 0:6])
                    e.bn_aggr(mv2[:], bnS[:, 6:12])
                    e.tensor_copy(sm1[:, t:t + 1], mv1[:, 0:1])
                    e.tensor_copy(sv1[:, t:t + 1], mv1[:, 1:2])
                    e.tensor_copy(sm2[:, t:t + 1], mv2[:, 0:1])
                    return e.tensor_copy(sv2[:, t:t + 1], mv2[:, 1:2])
                p.emit("vector", mk_bn,
                       waits=(first_wait,) if t == 0 else ())

            def mk_comb(e):
                NTT = n_tiles
                fil(e)
                e.tensor_tensor(smean[:, :NTT], sm1[:, :NTT], sm2[:, :NTT],
                                ALU.add)
                e.tensor_tensor(tmpa[:, :NTT], sm1[:, :NTT], sm1[:, :NTT],
                                ALU.mult)
                e.tensor_tensor(tmpb[:, :NTT], sm2[:, :NTT], sm2[:, :NTT],
                                ALU.mult)
                e.tensor_tensor(tmpc[:, :NTT], sv1[:, :NTT], sv2[:, :NTT],
                                ALU.add)
                e.tensor_scalar_mul(smean[:, :NTT], smean[:, :NTT], 0.5)
                e.tensor_tensor(tmpa[:, :NTT], tmpa[:, :NTT], tmpb[:, :NTT],
                                ALU.add)
                e.tensor_tensor(tmpd[:, :NTT], smean[:, :NTT], smean[:, :NTT],
                                ALU.mult)
                e.tensor_tensor(tmpa[:, :NTT], tmpa[:, :NTT], tmpc[:, :NTT],
                                ALU.add)
                e.tensor_scalar_add(tmpd[:, :NTT], tmpd[:, :NTT], -1e-5)
                fil(e)
                return e.scalar_tensor_tensor(
                    svar[:, :NTT], tmpa[:, :NTT], 0.5, tmpd[:, :NTT],
                    ALU.mult, ALU.subtract)
            t_var = p.emit("vector", mk_comb, inc="v")
            t_sd = p.emit("scalar", lambda e: e.activation(
                ssd[:, :NT], svar[:, :NT], AFT.Sqrt),
                waits=(("v", t_var),), inc="s")
            p.emit("vector", lambda e: e.reciprocal(
                rsig[:, :NT], ssd[:, :NT]), waits=(("s", t_sd),))
            t_ret = p.emit("vector", lambda e: (fil(e),
                e.scalar_tensor_tensor(
                    nbias[:, :NT], smean[:, :NT], -1.0, rsig[:, :NT],
                    ALU.mult, ALU.mult))[1], inc="v")
            return t_ret

        def ffn_phase(tag):
            fctx = ExitStack()
            with fctx:
                xqT = fctx.enter_context(
                    nc.sbuf_tensor(f"xqT{tag}", [128, NE, 128], BF16))
                hqT = fctx.enter_context(
                    nc.sbuf_tensor(f"hqT{tag}", [128, NI, ss * 128], BF16))
                sg = vbuf[:, I - 512:I]
                dnb = fctx.enter_context(
                    nc.sbuf_tensor(f"dnb{tag}", [128, ss * 128], BF16))
                wdq = fctx.enter_context(
                    nc.sbuf_tensor(f"wdq{tag}", [128, 2, NI // 2, 128], BF16))
                pg = fctx.enter_context(
                    nc.psum_tensor(f"pg{tag}", [128, 512], F32))
                pu = fctx.enter_context(
                    nc.psum_tensor(f"pu{tag}", [128, 512], F32))
                ptr = fctx.enter_context(
                    nc.psum_tensor(f"ptf{tag}", [128, 1024], BF16))
                pdn = fctx.enter_context(
                    nc.psum_tensor(f"pdn{tag}", [128, ss * 128], F32))

                t_ln = ln_stats_and_coeffs(("lx", x_loaded))

                sts = list(range(n_tiles // ss))
                wd_cons = [0, 0]
                t_qvh_free = 0
                for st in sts:
                    tiles = list(range(st * ss, (st + 1) * ss))
                    for t in tiles:
                        t_xln = p.emit("scalar", lambda e, t=t: e.activation(
                            qvh[:, :E], x_res[:, t], AFT.Identity,
                            bias=nbias[:, t:t + 1], scale=rsig[:, t:t + 1]),
                            waits=(("v", t_ln), ("v", t_qvh_free)), inc="s")
                        _, t_qx = quant(
                            E, qvh[:, :E], waits=(("s", t_xln),),
                            sh_mult=0.25)

                        def mk_trx(e):
                            r = None
                            for j in range(NE):
                                r = e.transpose(ptr[:, j * 128:(j + 1) * 128],
                                                qb2[:, j * 128:(j + 1) * 128],
                                                ident[:])
                            return r
                        t_trx = p.emit("tensor", mk_trx,
                                       waits=(("g", t_qx), ("s", p.n["s"])),
                                       inc="pe")
                        qstate["qb2_free"] = p.n["pe"]
                        t_evx = p.emit("scalar", lambda e: e.activation(
                            xqT[:].rearrange("p c n -> p (c n)"), ptr[:],
                            AFT.Identity),
                            waits=(("pe", t_trx),), inc="s")

                        for c0, cw in IC:
                            def mk_mm(e, c0=c0, cw=cw):
                                r = None
                                for j in range(NE):
                                    e.matmul(pg[:, :cw], xqT[:, j],
                                             wgq[:, j, c0:c0 + cw],
                                             start=(j == 0),
                                             stop=(j == NE - 1))
                                    r = e.matmul(pu[:, :cw], xqT[:, j],
                                                 wuq[:, j, c0:c0 + cw],
                                                 start=(j == 0),
                                                 stop=(j == NE - 1))
                                return r
                            t_mm = p.emit("tensor", mk_mm,
                                          waits=(("s", t_evx), ("v", p.n["v"]),
                                                 ("s", p.n["s"])), inc="pe")
                            t_silu = p.emit(
                                "scalar", lambda e, cw=cw: e.activation(
                                    sg[:, :cw], pg[:, :cw], AFT.Silu),
                                waits=(("pe", t_mm),), inc="s")
                            p.emit("vector",
                                   lambda e, c0=c0, cw=cw: e.tensor_tensor(
                                       qvh[:, c0:c0 + cw], sg[:, :cw],
                                       pu[:, :cw], ALU.mult),
                                   waits=(("s", t_silu), ("pe", t_mm)),
                                   inc="v")
                        t_qvh_free, t_qh = quant(I, qvh[:, :I], sh_mult=0.25,
                                                 amax_fil=True)
                        si = t - st * ss
                        for g0 in range(0, NI, 8):
                            gn = min(8, NI - g0)

                            def mk_trh(e, g0=g0, gn=gn):
                                r = None
                                for j in range(gn):
                                    r = e.transpose(
                                        ptr[:, j * 128:(j + 1) * 128],
                                        qb2[:, (g0 + j) * 128:
                                            (g0 + j + 1) * 128],
                                        ident[:])
                                return r
                            t_trh = p.emit("tensor", mk_trh,
                                           waits=(("g", t_qh),
                                                  ("s", p.n["s"])), inc="pe")

                            def mk_evh(e, g0=g0, gn=gn, si=si):
                                return e.activation(
                                    hqT[:, g0:g0 + gn,
                                        si * 128:(si + 1) * 128],
                                    ptr[:, :gn * 128]
                                    .rearrange("p (c n) -> p c n", n=128),
                                    AFT.Identity)
                            p.emit("scalar", mk_evh,
                                   waits=(("pe", t_trh),), inc="s")
                        qstate["qb2_free"] = p.n["pe"]

                    NIH = NI // 2
                    for hs in range(NH):
                        t_dn = 0
                        for half in range(2):
                            buf = (hs * 2 + half) % 2
                            lsem = "la" if buf == 0 else "lb"
                            t_wd = p.emit(
                                "sync",
                                lambda e, hs=hs, half=half, buf=buf:
                                e.dma_start(
                                    out=wdq[:, buf].rearrange(
                                        "p c n -> p (c n)"),
                                    in_=wd_scr[hs][:, half * NIH * 128:
                                                   (half + 1) * NIH * 128]),
                                waits=(("pe", wd_cons[buf]),
                                       ("wr", p.n["wr"])),
                                inc=lsem, k=16)

                            def mk_dn(e, buf=buf, half=half):
                                r = None
                                for jj in range(NIH):
                                    j = half * NIH + jj
                                    r = e.matmul(pdn[:], wdq[:, buf, jj],
                                                 hqT[:, j],
                                                 start=(j == 0),
                                                 stop=(j == NI - 1))
                                return r
                            t_dn = p.emit("tensor", mk_dn,
                                          waits=((lsem, t_wd),
                                                 ("v", p.n["v"]),
                                                 ("s", p.n["s"])), inc="pe")
                            wd_cons[buf] = t_dn
                        t_evd = p.emit("scalar", lambda e: e.activation(
                            dnb[:], pdn[:], AFT.Identity),
                            waits=(("pe", t_dn),), inc="s")

                        def mk_trb(e):
                            r = None
                            for si in range(ss):
                                r = e.transpose(
                                    ptr[:, si * 128:(si + 1) * 128],
                                    dnb[:, si * 128:(si + 1) * 128],
                                    ident[:])
                            return r
                        t_trb = p.emit("tensor", mk_trb,
                                       waits=(("s", t_evd), ("s", p.n["s"])),
                                       inc="pe")

                        def mk_add(e, hs=hs, st=st):
                            r = None
                            for si in range(ss):
                                t = st * ss + si
                                r = e.tensor_tensor(
                                    x_res[:, t, hs * 128:(hs + 1) * 128],
                                    x_res[:, t, hs * 128:(hs + 1) * 128],
                                    ptr[:, si * 128:(si + 1) * 128], ALU.add)
                            return r
                        p.emit("vector", mk_add, waits=(("pe", t_trb),),
                               inc="v")

            p.barrier(f"ffn{tag}_done")

        def attn_phase(stageb=True, bmode="full"):
            actx = ExitStack()
            with actx:
                xlT = actx.enter_context(
                    nc.sbuf_tensor("xlT", [128, 2, NE, 128], BF16))
                dump = actx.enter_context(
                    nc.sbuf_tensor("dump", [128, E], BF16))
                qT = actx.enter_context(nc.sbuf_tensor("qT", [64, T], BF16))
                kT = actx.enter_context(nc.sbuf_tensor("kT", [64, T], BF16))
                vv = actx.enter_context(
                    nc.sbuf_tensor("vv", [128, n_tiles, HD], BF16))
                vsc = actx.enter_context(
                    nc.sbuf_tensor("vsc", [128, HD], BF16))
                attnT = actx.enter_context(
                    nc.sbuf_tensor("attnT", [128, T], BF16))
                mx4 = actx.enter_context(nc.sbuf_tensor("mx4", [128, 4], F32))
                rmax = actx.enter_context(
                    nc.sbuf_tensor("rmax", [128, 1], F32))
                zrow = actx.enter_context(
                    nc.sbuf_tensor("zrow", [128, 1], F32))
                rz = actx.enter_context(nc.sbuf_tensor("rz", [128, 1], F32))

                t_ln = ln_stats_and_coeffs(("v", p.n["v"]))
                chunks = [(c, min(512, T - c * 512))
                          for c in range((T + 511) // 512)]
                nic = len(chunks)

                sctx = ExitStack()
                with sctx:
                    ptra = sctx.enter_context(
                        nc.psum_tensor("ptra", [128, 1024], BF16))
                    pj = sctx.enter_context(
                        nc.psum_tensor("pj", [128, 512], F32))
                    pv = sctx.enter_context(
                        nc.psum_tensor("pv", [128, 512], F32))
                    xl_cons = [0, 0]
                    for t in range(n_tiles):
                        xbuf = t % 2
                        t_x = p.emit("scalar", lambda e, t=t: e.activation(
                            dump[:], x_res[:, t], AFT.Identity,
                            bias=nbias[:, t:t + 1], scale=rsig[:, t:t + 1]),
                            waits=(("v", t_ln), ("pe", p.n["pe"])), inc="s")

                        def mk_tr(e):
                            r = None
                            for j in range(NE):
                                r = e.transpose(
                                    ptra[:, j * 128:(j + 1) * 128],
                                    dump[:, j * 128:(j + 1) * 128], ident[:])
                            return r
                        t_tr = p.emit("tensor", mk_tr,
                                      waits=(("s", t_x), ("s", p.n["s"])),
                                      inc="pe")
                        t_xl = p.emit("scalar",
                                      lambda e, xbuf=xbuf: e.activation(
                                          xlT[:, xbuf],
                                          ptra[:].rearrange(
                                              "p (c n) -> p c n", n=128),
                                          AFT.Identity),
                                      waits=(("pe", t_tr),
                                             ("pe", xl_cons[xbuf])), inc="s")

                        for j2, qkdst in ((0, qT), (1, kT)):
                            def mk_qk(e, j2=j2, xbuf=xbuf):
                                r = None
                                for j in range(NE):
                                    r = e.matmul(
                                        pj[0:HD, 0:128], qkvw[:, j2, j],
                                        xlT[:, xbuf, j],
                                        start=(j == 0), stop=(j == NE - 1))
                                return r
                            t_mm = p.emit("tensor", mk_qk,
                                          waits=(("s", t_xl),
                                                 ("v", p.qkv_ready),
                                                 ("s", p.n["s"])), inc="pe")
                            p.emit("scalar",
                                   lambda e, qkdst=qkdst, t=t:
                                   e.activation(
                                       qkdst[:, t * 128:(t + 1) * 128],
                                       pj[0:HD, 0:128], AFT.Identity),
                                   waits=(("pe", t_mm),), inc="s")
                        sl = t % 8

                        def mk_v(e, sl=sl, xbuf=xbuf):
                            r = None
                            for j in range(NE):
                                r = e.matmul(pv[:, sl * 64:(sl + 1) * 64],
                                             xlT[:, xbuf, j],
                                             qkvw[:, 2, j],
                                             start=(j == 0),
                                             stop=(j == NE - 1))
                            return r
                        t_mm = p.emit("tensor", mk_v,
                                      waits=(("s", t_xl), ("s", p.n["s"])),
                                      inc="pe")
                        xl_cons[xbuf] = t_mm
                        p.emit("scalar", lambda e, t=t, sl=sl: e.activation(
                            vv[:, t], pv[:, sl * 64:(sl + 1) * 64],
                            AFT.Identity),
                            waits=(("pe", t_mm),), inc="s")
                    p.barrier("attn_stageA")
                if not stageb:
                    return

                sctx2 = ExitStack()
                with sctx2:
                    poT = sctx2.enter_context(
                        nc.psum_tensor("poT", [64, T], F32))
                    poS = vbuf[0:64, 0:T]
                    scctx = ExitStack()
                    ps = scctx.enter_context(
                        nc.psum_tensor("ps", [128, 4, 512], F32))
                    for kt in range(n_tiles):
                        def mk_sc(e, kt=kt):
                            r = None
                            for c, cw in chunks:
                                r = e.matmul(
                                    ps[:, c, :cw],
                                    kT[:, kt * 128:(kt + 1) * 128],
                                    qT[:, c * 512:c * 512 + cw],
                                    start=True, stop=True)
                            return r
                        t_sc = p.emit("tensor", mk_sc,
                                      waits=(("v", p.n["v"]),
                                             ("s", p.n["s"])), inc="pe")

                        def mk_mx(e):
                            r = None
                            for c, cw in chunks:
                                r = e.tensor_reduce(mx4[:, c:c + 1],
                                                    ps[:, c, :cw],
                                                    AX.X, ALU.max)
                            fil(e)
                            return r
                        p.emit("vector", mk_mx,
                               waits=(("pe", t_sc), ("s", p.n["s"])))
                        t_nm = p.emit("vector", lambda e: e.tensor_reduce(
                            rmax[:], mx4[:, :nic], AX.X, ALU.max,
                            negate=True), inc="v")
                        t_exp = 0
                        pe_prev = p.n["pe"]
                        for c, cw in chunks:
                            t_exp = p.emit(
                                "scalar", lambda e, c=c, cw=cw: e.activation(
                                    attnT[:, c * 512:c * 512 + cw],
                                    ps[:, c, :cw], AFT.Exp, bias=rmax[:]),
                                waits=(("v", t_nm), ("pe", pe_prev)),
                                inc="s")
                        t_z = p.emit("vector", lambda e: e.tensor_reduce(
                            zrow[:], attnT[:], AX.X, ALU.add),
                            waits=(("s", t_exp),), inc="v")
                        p.emit("vector",
                               lambda e: (fil(e),
                                          e.reciprocal(rz[:], zrow[:]))[1])
                        t_vs = p.emit("vector",
                                      lambda e, kt=kt: (fil(e),
                                      e.tensor_scalar_mul(
                                          vsc[:], vv[:, kt], rz[:]))[1],
                                      waits=(("pe", p.n["pe"]),), inc="v")

                        if bmode == "noout":
                            continue

                        def mk_out(e, kt=kt):
                            r = None
                            for c, cw in chunks:
                                r = e.matmul(
                                    poT[:, c * 512:c * 512 + cw],
                                    vsc[:],
                                    attnT[:, c * 512:c * 512 + cw],
                                    start=(kt == 0),
                                    stop=(kt == n_tiles - 1))
                            return r
                        t_om = p.emit("tensor", mk_out,
                                      waits=(("v", t_vs),
                                             ("v", p.n["v"])), inc="pe")
                    scctx.close()
                    if bmode in ("noout", "nopost"):
                        p.barrier("attn_done")
                        return
                    def mk_evo(e):
                        r = None
                        for c, cw in chunks:
                            r = e.activation(poS[:, c * 512:c * 512 + cw],
                                             poT[:, c * 512:c * 512 + cw],
                                             AFT.Identity)
                        return r
                    t_evo = p.emit("scalar", mk_evo,
                                   waits=(("pe", t_om),), inc="s")
                    if bmode == "evoonly":
                        p.barrier("attn_done")
                        return
                    ptb = sctx2.enter_context(
                        nc.psum_tensor("ptb", [128, n_tiles, HD], F32))
                    t_tb = 0
                    for it in range(n_tiles):
                        t_tb = p.emit("tensor", lambda e, it=it: e.transpose(
                            ptb[:, it],
                            poS[:, it * 128:(it + 1) * 128],
                            identf[0:64, 0:64]),
                            waits=(("s", t_evo),), inc="pe")
                    if bmode != "notradd":
                        for it in range(n_tiles):
                            def mk_radd(e, it=it):
                                pb = ptb[:, it].unsqueeze(1).broadcast_to(
                                    (128, 16, HD))
                                xr = x_res[:, it].rearrange(
                                    "p (r c) -> p r c", c=HD)
                                return e.tensor_tensor(xr, xr, pb, ALU.add)
                            p.emit("vector", mk_radd,
                                   waits=(("pe", t_tb),), inc="v")
                    p.barrier("attn_done")

        if stop_after != "w":
            ffn_phase(1)
            if stop_after != "ffn1":
                bmode = "full"
                if stop_after in ("attn_noout", "attn_nopost",
                                  "attn_evoonly", "attn_notradd"):
                    bmode = stop_after.split("_")[1]
                attn_phase(stageb=(stop_after != "attnA"), bmode=bmode)
                if stop_after is None or stop_after == "ffn3":
                    ffn_phase(3)

        t_all = p.n["v"]
        p.emit("gpsimd", lambda e: e.dma_start(
            out=out[:].rearrange("(t p) e -> p t e", p=128), in_=x_res[:]),
            waits=(("v", t_all),), inc="wr", k=16)
        fin = p.n["wr"]
        p.emit("gpsimd", lambda e: None, waits=(("wr", fin),))

        with nc.Block() as block:
            p.run(block)

    return nc



_CACHE = {}


def _get_nc(n_tiles=16, ss=2):
    key = (n_tiles, ss)
    if key not in _CACHE:
        nc = build_program(n_tiles=n_tiles, ss=ss)
        lower_extended_insts(nc)
        _CACHE[key] = nc
    return _CACHE[key]


def kernel(L0_attn_x, L0_attn_Wq, L0_attn_Wk, L0_attn_Wv,
           L0_ffn_w_gate, L0_ffn_w_up, L0_ffn_w_down):
    x = np.ascontiguousarray(np.asarray(L0_attn_x, dtype=np.float32))
    wqT = np.ascontiguousarray(np.asarray(L0_attn_Wq, dtype=np.float32).T)
    wkT = np.ascontiguousarray(np.asarray(L0_attn_Wk, dtype=np.float32).T)
    wvT = np.ascontiguousarray(np.asarray(L0_attn_Wv, dtype=np.float32).T)
    wg = np.ascontiguousarray(np.asarray(L0_ffn_w_gate, dtype=np.float32))
    wu = np.ascontiguousarray(np.asarray(L0_ffn_w_up, dtype=np.float32))
    wd = np.ascontiguousarray(np.asarray(L0_ffn_w_down, dtype=np.float32))
    B = x.shape[0]
    nc = _get_nc()
    in_maps = [{"x": x[b], "wqT": wqT, "wkT": wkT, "wvT": wvT,
                "w_gate": wg, "w_up": wu, "w_down": wd} for b in range(B)]
    res = run_bass_kernel_spmd(nc, in_maps, core_ids=list(range(B)))
    return np.stack([np.asarray(res.results[b]["out"]) for b in range(B)],
                    axis=0).astype(np.float32)


# version 05
# speedup vs baseline: 1.1711x; 1.0159x over previous
import numpy as np

import concourse.bass as bass
import concourse.mybir as mybir
from concourse.bass_utils import run_bass_kernel_spmd
from concourse.library_overlay import lower_extended_insts
from concourse import dve_ops
from concourse.dve_op import DveOp
from concourse.dve_spec import (
    Spec, Src0, Src1, C0, C1, C2, Zero, One, maxx, minn, lower as dve_lower,
)

F32 = mybir.dt.float32
BF16 = mybir.dt.bfloat16
ALU = mybir.AluOpType
AFT = mybir.ActivationFunctionType
AX = mybir.AxisListType

MAGIC = 12582912.0


def _f(in0):
    return np.float64 if in0.dtype == np.float64 else np.float32


def _ref_v2(in0, in1, s0, s1, imm2):
    p = in0.astype(_f(in0)) * in1
    return (p + p).astype(_f(in0))


def _rne(v, f):
    if f is np.float64:
        return np.round(v)
    m = np.float32(MAGIC)
    return ((v.astype(np.float32) + m) - m).astype(np.float32)


def _ref_qs(in0, in1, s0, s1, imm2):
    f = _f(in0)
    u = np.abs(in0.astype(f))
    return ((u > s0).astype(f) + (u > s1).astype(f)
            + (u > imm2).astype(f) + (u > imm2).astype(f)).astype(f)


def _ref_qt(in0, in1, s0, s1, imm2):
    f = _f(in0)
    v = in0.astype(f)
    a = np.clip(_rne(v, f), -s1, s1)
    return (a * (in1.astype(f) + 2.0)).astype(f)


def _qs_body():
    v = Src0
    u = maxx(v, Zero - v)
    u5 = u > C0
    u7 = u > C1
    u10 = u > C2
    return ((u5 + u7) + u10) + u10


def _qt_body():
    v = Src0
    a = minn(maxx((v + C0) - C0, Zero - C1), C1)
    return a * (Src1 + (One + One))


def _register_ops():
    specs = {
        "ANT_V2": ((lambda p: p + p)(Src0 * Src1), _ref_v2, 28),
        "ANT_QS": (_qs_body(), _ref_qs, 29),
        "ANT_QT": (_qt_body(), _ref_qt, 30),
    }
    out = {}
    have = {o.name: o for o in dve_ops.OPS}
    for name, (body, ref, row) in specs.items():
        if name in have:
            out[name] = have[name]
            continue
        spec = Spec(body=body)
        compiled = dve_lower(spec, ver="v3", perf_en=False, name=name)
        op = DveOp(name, spec, subdim=False,
                   uops_sha={"v3": compiled.sha("v3")}, reference=ref)
        dve_ops.OPS.append(op)
        dve_ops._SUB_OPCODE_FOR_NAME[name] = row
        dve_ops.CUSTOM_DVE_SPECS[name] = spec
        dve_ops.CUSTOM_DVE_REFS[name] = ref
        out[name] = op
    return out


OPS = _register_ops()


ENGINES = ("sync", "gpsimd", "vector", "scalar", "tensor")


class Plan:
    def __init__(self, nc):
        self.nc = nc
        self.q = {e: [] for e in ENGINES}
        self.sem = {}
        self.n = {}

    def mksem(self, ctx, *names):
        for nm in names:
            self.sem[nm] = ctx.enter_context(self.nc.semaphore(nm))
            self.n[nm] = 0

    def emit(self, eng, make, waits=(), inc=None, k=1):
        w = tuple(waits)
        if inc is not None:
            self.n[inc] += k

        def fn(e, make=make, w=w, inc=inc, k=k):
            for s, v in w:
                if v > 0:
                    e.wait_ge(self.sem[s], v)
            r = make(e)
            if inc is not None:
                if r is None:
                    r = e.nop()
                r.then_inc(self.sem[inc], k)
        self.q[eng].append(fn)
        return self.n[inc] if inc is not None else None

    def barrier(self, tag):
        tgt = self.n["ph"] + len(ENGINES)
        fence = self.fence_makers
        self.emit("sync", fence["sync"],
                  waits=tuple((s, self.n[s]) for s in ("lx", "lq", "la", "lb")),
                  inc="ph")
        self.emit("gpsimd", fence["gpsimd"], waits=(("wr", self.n["wr"]),),
                  inc="ph")
        for e in ("vector", "scalar", "tensor"):
            self.emit(e, fence[e], inc="ph")
        for e in ENGINES:
            self.emit(e, lambda en: None, waits=(("ph", tgt),))

    def run(self, block):
        for eng in ENGINES:
            getattr(block, eng)(self._runner(eng))

    def _runner(self, eng):
        fns = self.q[eng]

        def go(e):
            for fn in fns:
                fn(e)
        return go



def build_program(n_tiles=16, ss=2, stop_after=None, taps=()):
    from contextlib import ExitStack

    T = n_tiles * 128
    E, I, HD = 1024, 2816, 64
    NE = E // 128
    NI = I // 128
    IC = [(c * 512, min(512, I - c * 512)) for c in range((I + 511) // 512)]
    NH = E // 128
    assert n_tiles % ss == 0

    nc = bass.Bass("TRN2", target_bir_lowering=False,
                   dynamic_dma_scratch_size=8192,
                   detect_race_conditions=False)

    x_in = nc.dram_tensor("x", [T, E], F32, kind="ExternalInput").ap()
    wq_in = nc.dram_tensor("wqT", [E, HD], F32, kind="ExternalInput").ap()
    wk_in = nc.dram_tensor("wkT", [E, HD], F32, kind="ExternalInput").ap()
    wv_in = nc.dram_tensor("wvT", [E, HD], F32, kind="ExternalInput").ap()
    wg_in = nc.dram_tensor("w_gate", [I, E], F32, kind="ExternalInput").ap()
    wu_in = nc.dram_tensor("w_up", [I, E], F32, kind="ExternalInput").ap()
    wd_in = nc.dram_tensor("w_down", [E, I], F32, kind="ExternalInput").ap()
    out = nc.dram_tensor("out", [T, E], F32, kind="ExternalOutput").ap()
    wd_scr = [nc.dram_tensor(f"wd_scr{j}", [128, NI * 128], BF16,
                             kind="ExternalOutput").ap() for j in range(NH)]

    tap_out = {}
    def mktap(name, shape, dtype):
        tap_out[name] = nc.dram_tensor("tap_" + name, shape, dtype,
                                       kind="ExternalOutput").ap()
        return tap_out[name]

    ctx = ExitStack()
    with ctx:
        p = Plan(nc)
        p.mksem(ctx, "lx", "lq", "la", "lb", "wr", "v", "s", "pe", "g", "ph")

        def fil(e):
            e.tensor_copy(filA[:], filB[:])

        p.fence_makers = {
            "sync": lambda e: e.nop(),
            "gpsimd": lambda e: e.engine_nop(),
            "vector": lambda e: e.tensor_copy(filA[:], filB[:]),
            "scalar": lambda e: e.activation(filA[:], filB[:], AFT.Identity),
            "tensor": lambda e: e.nop(),
        }

        def tap(name, src_ap, after_sem):
            if name not in taps:
                return
            d = mktap(name, src_ap.shape, src_ap.dtype)
            p.emit("gpsimd", lambda e: e.dma_start(out=d[:], in_=src_ap),
                   waits=(after_sem,), inc="wr", k=16)

        def sb(name, shape, dtype):
            return ctx.enter_context(nc.sbuf_tensor(name, shape, dtype))

        ident = sb("ident", [128, 128], BF16)
        identf = sb("identf", [128, 128], F32)
        x_res = sb("x_res", [128, n_tiles, E], F32)
        wgq = sb("wgq", [128, NE, I], BF16)
        wuq = sb("wuq", [128, NE, I], BF16)
        svar = sb("svar", [128, n_tiles], F32)
        ssd = sb("ssd", [128, n_tiles], F32)
        rsig = sb("rsig", [128, n_tiles], F32)
        nbias = sb("nbias", [128, n_tiles], F32)
        smean = sb("smean", [128, n_tiles], F32)

        bnS = sb("bnS", [128, 12], F32)
        filA = sb("filA", [128, 1], F32)
        filB = sb("filB", [128, 1], F32)
        mv1 = sb("mv1", [128, 2], F32)
        mv2 = sb("mv2", [128, 2], F32)
        sm1 = sb("sm1", [128, n_tiles], F32)
        sv1 = sb("sv1", [128, n_tiles], F32)
        sm2 = sb("sm2", [128, n_tiles], F32)
        sv2 = sb("sv2", [128, n_tiles], F32)
        tmpa = sb("tmpa", [128, n_tiles], F32)
        tmpb = sb("tmpb", [128, n_tiles], F32)
        tmpc = sb("tmpc", [128, n_tiles], F32)
        tmpd = sb("tmpd", [128, n_tiles], F32)

        qvh = sb("qvh", [128, I], F32)
        qvx = sb("qvx", [128, E], F32)
        vbuf = sb("vbuf", [128, I], F32)
        qt_o = sb("qt_o", [128, I], BF16)
        qb2 = sb("qb2", [128, I], BF16)
        qb2x = sb("qb2x", [128, E], BF16)
        bmax = sb("bmax", [128, I // 16], F32)
        scl = sb("scl", [128, I // 16], F32)
        rs2 = sb("rs2", [128, I // 16], F32)
        shl = sb("shl", [128, 2, I // 16], F32)

        p.emit("vector", lambda e: e.memset(filB[:], 0.0))
        t_ms = p.emit("gpsimd", lambda e: e.memset(ident[:], 0.0), inc="g")
        p.emit("gpsimd", lambda e: e.affine_select(
            out=ident[:], in_=ident[:],
            compare_op=ALU.not_equal, fill=1.0, base=0,
            pattern=[[-1, 128]], channel_multiplier=1),
            waits=(("g", t_ms),), inc="s")
        t_msf = p.emit("gpsimd", lambda e: e.memset(identf[:], 0.0), inc="g")
        p.emit("gpsimd", lambda e: e.affine_select(
            out=identf[:], in_=identf[:],
            compare_op=ALU.not_equal, fill=1.0, base=0,
            pattern=[[-1, 128]], channel_multiplier=1),
            waits=(("g", t_msf),), inc="s")
        ident_ready = p.n["s"]

        p.emit("sync", lambda e: e.dma_start(
            out=x_res[:], in_=x_in[:].rearrange("(t p) e -> p t e", p=128)),
            inc="lx", k=16)
        x_loaded = p.n["lx"]
        p.tap_fn = tap

        qstate = {
            "par": 0,
            "mulb": [0, 0],
            "mulb_last": 0,
            "free_h": 0,
            "free_x": 0,
        }

        def quant(width, src_ap, waits=(), sh_mult=0.25, amax_fil=False,
                  stream="h"):
            nb = width // 16
            par = qstate["par"]
            qstate["par"] ^= 1
            qdst = qb2x if stream == "x" else qb2
            s3 = src_ap.rearrange("p (b k) -> p b k", k=16)
            p.emit("vector", lambda e: ((fil(e) if amax_fil else None),
                e.tensor_reduce(bmax[:, :nb], s3, AX.X, ALU.max,
                                apply_absolute_value=True))[1], waits=waits)
            p.emit("vector", lambda e: (fil(e), e.tensor_scalar(
                scl[:, :nb], bmax[:, :nb], 1.0 / 6.0, 1e-12,
                ALU.mult, ALU.max))[1])
            p.emit("vector", lambda e: (fil(e), e.reciprocal(
                rs2[:, :nb], scl[:, :nb]))[1])
            p.emit("vector", lambda e, par=par: e.tensor_scalar_mul(
                shl[:, par, :nb], scl[:, :nb], sh_mult),
                waits=(("g", qstate["mulb"][par]),))
            rsb = rs2[:, :nb].unsqueeze(2).broadcast_to((128, nb, 16))
            v3 = vbuf[:, :width].rearrange("p (b k) -> p b k", k=16)
            p.emit("vector", lambda e: e._custom_dve(
                OPS["ANT_V2"], out=v3, in0=s3, in1=rsb))
            p.emit("vector", lambda e: (fil(e), e._custom_dve(
                OPS["ANT_QS"], out=src_ap, in0=vbuf[:, :width],
                s0=5.0, s1=7.0, imm2=10.0))[1])
            t_qt = p.emit("vector", lambda e: (fil(e), e._custom_dve(
                OPS["ANT_QT"], out=qt_o[:, :width], in0=vbuf[:, :width],
                in1=src_ap, s0=MAGIC, s1=4.0))[1],
                waits=(("g", qstate["mulb_last"]),), inc="v")
            shb = shl[:, par, :nb].unsqueeze(2).broadcast_to((128, nb, 16))
            q3 = qdst[:, :width].rearrange("p (b k) -> p b k", k=16)
            t3 = qt_o[:, :width].rearrange("p (b k) -> p b k", k=16)
            t_q = p.emit("gpsimd", lambda e: e.tensor_tensor(
                q3, t3, shb, ALU.mult),
                waits=(("v", t_qt), ("pe", qstate["free_" + stream])),
                inc="g")
            qstate["mulb"][par] = t_q
            qstate["mulb_last"] = t_q
            return t_qt, t_q

        wctx = ExitStack()
        with wctx:
            wld = wctx.enter_context(
                nc.sbuf_tensor("wld", [128, 2, 1408], F32))
            wstg = wctx.enter_context(nc.sbuf_tensor("wstg", [128, 1024], BF16))
            ptrw = wctx.enter_context(nc.psum_tensor("ptrw", [128, 1024], BF16))

            cons_v = [0, 0]

            def wtile(buf, src_dram_ap, width, dst_kind, dst_idx, sh_mult,
                      goff=0):
                lsem = "la" if buf == 0 else "lb"
                t_ld = p.emit("sync", lambda e: e.dma_start(
                    out=wld[:, buf, :width], in_=src_dram_ap),
                    waits=(("v", cons_v[buf]),), inc=lsem, k=16)
                t_qt, t_q = quant(width, wld[:, buf, :width],
                                  waits=((lsem, t_ld),), sh_mult=sh_mult)
                cons_v[buf] = t_qt
                nblk = width // 128
                for g0 in range(0, nblk, 8):
                    gn = min(8, nblk - g0)

                    def mk_tr(e, g0=g0, gn=gn):
                        r = None
                        for j in range(gn):
                            r = e.transpose(
                                ptrw[:, j * 128:(j + 1) * 128],
                                qb2[:, (g0 + j) * 128:(g0 + j + 1) * 128],
                                ident[:])
                        return r
                    t_tr = p.emit("tensor", mk_tr,
                                  waits=(("g", t_q), ("s", ident_ready),
                                         ("wr", p.n["wr"]), ("s", p.n["s"])),
                                  inc="pe")
                    if dst_kind in ("gate", "up"):
                        dst = wgq if dst_kind == "gate" else wuq

                        def mk_ev(e, g0=g0, gn=gn, dst=dst, dst_idx=dst_idx):
                            return e.activation(
                                dst[:, g0:g0 + gn,
                                    dst_idx * 128:(dst_idx + 1) * 128],
                                ptrw[:, :gn * 128]
                                .rearrange("p (c n) -> p c n", n=128),
                                AFT.Identity)
                        p.emit("scalar", mk_ev, waits=(("pe", t_tr),), inc="s")
                    else:
                        def mk_ev(e, gn=gn):
                            return e.activation(wstg[:, :gn * 128],
                                                ptrw[:, :gn * 128],
                                                AFT.Identity)
                        t_ev = p.emit("scalar", mk_ev,
                                      waits=(("pe", t_tr),), inc="s")

                        def mk_st(e, g0=g0 + goff, gn=gn, dst_idx=dst_idx):
                            return e.dma_start(
                                out=wd_scr[dst_idx][:,
                                           g0 * 128:(g0 + gn) * 128],
                                in_=wstg[:, :gn * 128])
                        p.emit("gpsimd", mk_st, waits=(("s", t_ev),),
                               inc="wr", k=16)
                qstate["free_h"] = p.n["pe"]

            wg_t = wg_in.rearrange("(c p) e -> c p e", p=128)
            wu_t = wu_in.rearrange("(c p) e -> c p e", p=128)
            wd_t = wd_in.rearrange("(c p) i -> c p i", p=128)
            bi = 0
            for i in range(NI):
                wtile(bi % 2, wg_t[i], E, "gate", i, 0.25); bi += 1
            for i in range(NI):
                wtile(bi % 2, wu_t[i], E, "up", i, 0.25); bi += 1
            for i in range(NH):
                for h in range(2):
                    wtile(bi % 2, wd_t[i][:, h * 1408:(h + 1) * 1408],
                          1408, "down", i, 0.125, goff=h * 11)
                    bi += 1

            p.barrier("w_done")

        def ln_stats_and_coeffs(first_wait):
            NT = n_tiles
            for t in range(n_tiles):
                def mk_bn(e, t=t):
                    e.bn_stats(bnS[:, 0:6], x_res[:, t, 0:512])
                    e.bn_stats(bnS[:, 6:12], x_res[:, t, 512:1024])
                    e.bn_aggr(mv1[:], bnS[:, 0:6])
                    e.bn_aggr(mv2[:], bnS[:, 6:12])
                    e.tensor_copy(sm1[:, t:t + 1], mv1[:, 0:1])
                    e.tensor_copy(sv1[:, t:t + 1], mv1[:, 1:2])
                    e.tensor_copy(sm2[:, t:t + 1], mv2[:, 0:1])
                    return e.tensor_copy(sv2[:, t:t + 1], mv2[:, 1:2])
                p.emit("vector", mk_bn,
                       waits=(first_wait,) if t == 0 else ())

            def mk_comb(e):
                NTT = n_tiles
                fil(e)
                e.tensor_tensor(smean[:, :NTT], sm1[:, :NTT], sm2[:, :NTT],
                                ALU.add)
                e.tensor_tensor(tmpa[:, :NTT], sm1[:, :NTT], sm1[:, :NTT],
                                ALU.mult)
                e.tensor_tensor(tmpb[:, :NTT], sm2[:, :NTT], sm2[:, :NTT],
                                ALU.mult)
                e.tensor_tensor(tmpc[:, :NTT], sv1[:, :NTT], sv2[:, :NTT],
                                ALU.add)
                e.tensor_scalar_mul(smean[:, :NTT], smean[:, :NTT], 0.5)
                e.tensor_tensor(tmpa[:, :NTT], tmpa[:, :NTT], tmpb[:, :NTT],
                                ALU.add)
                e.tensor_tensor(tmpd[:, :NTT], smean[:, :NTT], smean[:, :NTT],
                                ALU.mult)
                e.tensor_tensor(tmpa[:, :NTT], tmpa[:, :NTT], tmpc[:, :NTT],
                                ALU.add)
                e.tensor_scalar_add(tmpd[:, :NTT], tmpd[:, :NTT], -1e-5)
                fil(e)
                return e.scalar_tensor_tensor(
                    svar[:, :NTT], tmpa[:, :NTT], 0.5, tmpd[:, :NTT],
                    ALU.mult, ALU.subtract)
            t_var = p.emit("vector", mk_comb, inc="v")
            t_sd = p.emit("scalar", lambda e: e.activation(
                ssd[:, :NT], svar[:, :NT], AFT.Sqrt),
                waits=(("v", t_var),), inc="s")
            p.emit("vector", lambda e: e.reciprocal(
                rsig[:, :NT], ssd[:, :NT]), waits=(("s", t_sd),))
            t_ret = p.emit("vector", lambda e: (fil(e),
                e.scalar_tensor_tensor(
                    nbias[:, :NT], smean[:, :NT], -1.0, rsig[:, :NT],
                    ALU.mult, ALU.mult))[1], inc="v")
            return t_ret

        def ffn_phase(tag):
            fctx = ExitStack()
            with fctx:
                xqT = fctx.enter_context(
                    nc.sbuf_tensor(f"xqT{tag}", [128, NE, 128], BF16))
                hqT = fctx.enter_context(
                    nc.sbuf_tensor(f"hqT{tag}", [128, NI, ss * 128], BF16))
                sg = vbuf[:, I - 512:I]
                dnb = fctx.enter_context(
                    nc.sbuf_tensor(f"dnb{tag}", [128, ss * 128], BF16))
                wdq = fctx.enter_context(
                    nc.sbuf_tensor(f"wdq{tag}", [128, 2, NI // 2, 128], BF16))
                pg = fctx.enter_context(
                    nc.psum_tensor(f"pg{tag}", [128, 512], F32))
                pu = fctx.enter_context(
                    nc.psum_tensor(f"pu{tag}", [128, 512], F32))
                ptr = fctx.enter_context(
                    nc.psum_tensor(f"ptf{tag}", [128, 1024], BF16))
                pdn = fctx.enter_context(
                    nc.psum_tensor(f"pdn{tag}", [128, ss * 128], F32))

                t_ln = ln_stats_and_coeffs(("lx", x_loaded))

                sts = list(range(n_tiles // ss))
                wd_cons = [0, 0]
                xq_done = {}
                qvx_free = [0]

                def emit_xq(t):
                    t_xln = p.emit("scalar", lambda e, t=t: e.activation(
                        qvx[:, :E], x_res[:, t], AFT.Identity,
                        bias=nbias[:, t:t + 1], scale=rsig[:, t:t + 1]),
                        waits=(("v", t_ln), ("v", qvx_free[0])), inc="s")
                    t_qt, t_qx = quant(
                        E, qvx[:, :E], waits=(("s", t_xln),),
                        sh_mult=0.25, stream="x")
                    qvx_free[0] = t_qt
                    xq_done[t] = t_qx

                for st in sts:
                    tiles = list(range(st * ss, (st + 1) * ss))
                    for t in tiles:
                        if t not in xq_done:
                            emit_xq(t)
                        t_qx = xq_done[t]

                        def mk_trx(e):
                            r = None
                            for j in range(NE):
                                r = e.transpose(ptr[:, j * 128:(j + 1) * 128],
                                                qb2x[:, j * 128:(j + 1) * 128],
                                                ident[:])
                            return r
                        t_trx = p.emit("tensor", mk_trx,
                                       waits=(("g", t_qx), ("s", p.n["s"]),
                                              ("v", p.n["v"])),
                                       inc="pe")
                        qstate["free_x"] = p.n["pe"]
                        t_evx = p.emit("scalar", lambda e: e.activation(
                            xqT[:].rearrange("p c n -> p (c n)"), ptr[:],
                            AFT.Identity),
                            waits=(("pe", t_trx),), inc="s")

                        for c0, cw in IC:
                            def mk_mm(e, c0=c0, cw=cw):
                                r = None
                                for j in range(NE):
                                    e.matmul(pg[:, :cw], xqT[:, j],
                                             wgq[:, j, c0:c0 + cw],
                                             start=(j == 0),
                                             stop=(j == NE - 1))
                                    r = e.matmul(pu[:, :cw], xqT[:, j],
                                                 wuq[:, j, c0:c0 + cw],
                                                 start=(j == 0),
                                                 stop=(j == NE - 1))
                                return r
                            t_mm = p.emit("tensor", mk_mm,
                                          waits=(("s", t_evx), ("v", p.n["v"]),
                                                 ("s", p.n["s"])), inc="pe")
                            t_silu = p.emit(
                                "scalar", lambda e, cw=cw: e.activation(
                                    sg[:, :cw], pg[:, :cw], AFT.Silu),
                                waits=(("pe", t_mm),), inc="s")
                            p.emit("vector",
                                   lambda e, c0=c0, cw=cw: e.tensor_tensor(
                                       qvh[:, c0:c0 + cw], sg[:, :cw],
                                       pu[:, :cw], ALU.mult),
                                   waits=(("s", t_silu), ("pe", t_mm)),
                                   inc="v")
                        if t + 1 < n_tiles:
                            emit_xq(t + 1)
                        _, t_qh = quant(I, qvh[:, :I], sh_mult=0.25,
                                        amax_fil=True)
                        si = t - st * ss
                        for g0 in range(0, NI, 8):
                            gn = min(8, NI - g0)

                            def mk_trh(e, g0=g0, gn=gn):
                                r = None
                                for j in range(gn):
                                    r = e.transpose(
                                        ptr[:, j * 128:(j + 1) * 128],
                                        qb2[:, (g0 + j) * 128:
                                            (g0 + j + 1) * 128],
                                        ident[:])
                                return r
                            t_trh = p.emit("tensor", mk_trh,
                                           waits=(("g", t_qh),
                                                  ("s", p.n["s"]),
                                                  ("v", p.n["v"])), inc="pe")

                            def mk_evh(e, g0=g0, gn=gn, si=si):
                                return e.activation(
                                    hqT[:, g0:g0 + gn,
                                        si * 128:(si + 1) * 128],
                                    ptr[:, :gn * 128]
                                    .rearrange("p (c n) -> p c n", n=128),
                                    AFT.Identity)
                            p.emit("scalar", mk_evh,
                                   waits=(("pe", t_trh),), inc="s")
                        qstate["free_h"] = p.n["pe"]

                    NIH = NI // 2
                    for hs in range(NH):
                        t_dn = 0
                        for half in range(2):
                            buf = (hs * 2 + half) % 2
                            lsem = "la" if buf == 0 else "lb"
                            t_wd = p.emit(
                                "sync",
                                lambda e, hs=hs, half=half, buf=buf:
                                e.dma_start(
                                    out=wdq[:, buf].rearrange(
                                        "p c n -> p (c n)"),
                                    in_=wd_scr[hs][:, half * NIH * 128:
                                                   (half + 1) * NIH * 128]),
                                waits=(("pe", wd_cons[buf]),
                                       ("wr", p.n["wr"])),
                                inc=lsem, k=16)

                            def mk_dn(e, buf=buf, half=half):
                                r = None
                                for jj in range(NIH):
                                    j = half * NIH + jj
                                    r = e.matmul(pdn[:], wdq[:, buf, jj],
                                                 hqT[:, j],
                                                 start=(j == 0),
                                                 stop=(j == NI - 1))
                                return r
                            t_dn = p.emit("tensor", mk_dn,
                                          waits=((lsem, t_wd),
                                                 ("v", p.n["v"]),
                                                 ("s", p.n["s"])), inc="pe")
                            wd_cons[buf] = t_dn
                        t_evd = p.emit("scalar", lambda e: e.activation(
                            dnb[:], pdn[:], AFT.Identity),
                            waits=(("pe", t_dn),), inc="s")

                        def mk_trb(e):
                            r = None
                            for si in range(ss):
                                r = e.transpose(
                                    ptr[:, si * 128:(si + 1) * 128],
                                    dnb[:, si * 128:(si + 1) * 128],
                                    ident[:])
                            return r
                        t_trb = p.emit("tensor", mk_trb,
                                       waits=(("s", t_evd), ("s", p.n["s"]),
                                              ("v", p.n["v"])),
                                       inc="pe")

                        def mk_add(e, hs=hs, st=st):
                            r = None
                            for si in range(ss):
                                t = st * ss + si
                                r = e.tensor_tensor(
                                    x_res[:, t, hs * 128:(hs + 1) * 128],
                                    x_res[:, t, hs * 128:(hs + 1) * 128],
                                    ptr[:, si * 128:(si + 1) * 128], ALU.add)
                            return r
                        p.emit("vector", mk_add, waits=(("pe", t_trb),),
                               inc="v")

            p.barrier(f"ffn{tag}_done")

        def attn_phase(stageb=True, bmode="full"):
            actx = ExitStack()
            with actx:
                xlT = actx.enter_context(
                    nc.sbuf_tensor("xlT", [128, 2, NE, 128], BF16))
                dump = qb2[:, 0:E]
                vv = qb2x[:, 0:n_tiles * HD].rearrange(
                    "p (t d) -> p t d", d=HD)
                attnT = qt_o[:, 0:T]
                qT = actx.enter_context(nc.sbuf_tensor("qT", [64, T], BF16))
                kT = actx.enter_context(nc.sbuf_tensor("kT", [64, T], BF16))
                vsc = actx.enter_context(
                    nc.sbuf_tensor("vsc", [128, HD], BF16))
                mx4 = actx.enter_context(nc.sbuf_tensor("mx4", [128, 4], F32))
                rmax = actx.enter_context(
                    nc.sbuf_tensor("rmax", [128, 1], F32))
                zrow = actx.enter_context(
                    nc.sbuf_tensor("zrow", [128, 1], F32))
                rz = actx.enter_context(nc.sbuf_tensor("rz", [128, 1], F32))
                qkvw = actx.enter_context(
                    nc.sbuf_tensor("qkvw", [128, 3, NE, HD], BF16))

                wtmp = vbuf[:, 0:NE * HD].rearrange("p (c d) -> p c d", d=HD)
                qkv_ready = 0
                for j, win in enumerate((wq_in, wk_in, wv_in)):
                    t = p.emit("sync", lambda e, win=win: e.dma_start(
                        out=wtmp,
                        in_=win[:].rearrange("(c p) d -> p c d", p=128)),
                        waits=(("v", qkv_ready),), inc="lq", k=16)
                    qkv_ready = p.emit(
                        "vector",
                        lambda e, j=j: e.tensor_copy(qkvw[:, j], wtmp),
                        waits=(("lq", t),), inc="v")

                t_ln = ln_stats_and_coeffs(("v", p.n["v"]))
                chunks = [(c, min(512, T - c * 512))
                          for c in range((T + 511) // 512)]
                nic = len(chunks)

                sctx = ExitStack()
                with sctx:
                    ptra = sctx.enter_context(
                        nc.psum_tensor("ptra", [128, 1024], BF16))
                    pj = sctx.enter_context(
                        nc.psum_tensor("pj", [128, 512], F32))
                    pv = sctx.enter_context(
                        nc.psum_tensor("pv", [128, 512], F32))
                    xl_cons = [0, 0]
                    for t in range(n_tiles):
                        xbuf = t % 2
                        t_x = p.emit("scalar", lambda e, t=t: e.activation(
                            dump[:], x_res[:, t], AFT.Identity,
                            bias=nbias[:, t:t + 1], scale=rsig[:, t:t + 1]),
                            waits=(("v", t_ln), ("pe", p.n["pe"])), inc="s")

                        def mk_tr(e):
                            r = None
                            for j in range(NE):
                                r = e.transpose(
                                    ptra[:, j * 128:(j + 1) * 128],
                                    dump[:, j * 128:(j + 1) * 128], ident[:])
                            return r
                        t_tr = p.emit("tensor", mk_tr,
                                      waits=(("s", t_x), ("s", p.n["s"])),
                                      inc="pe")
                        t_xl = p.emit("scalar",
                                      lambda e, xbuf=xbuf: e.activation(
                                          xlT[:, xbuf],
                                          ptra[:].rearrange(
                                              "p (c n) -> p c n", n=128),
                                          AFT.Identity),
                                      waits=(("pe", t_tr),
                                             ("pe", xl_cons[xbuf])), inc="s")

                        for j2, qkdst in ((0, qT), (1, kT)):
                            def mk_qk(e, j2=j2, xbuf=xbuf):
                                r = None
                                for j in range(NE):
                                    r = e.matmul(
                                        pj[0:HD, 0:128], qkvw[:, j2, j],
                                        xlT[:, xbuf, j],
                                        start=(j == 0), stop=(j == NE - 1))
                                return r
                            t_mm = p.emit("tensor", mk_qk,
                                          waits=(("s", t_xl),
                                                 ("v", qkv_ready),
                                                 ("s", p.n["s"])), inc="pe")
                            p.emit("scalar",
                                   lambda e, qkdst=qkdst, t=t:
                                   e.activation(
                                       qkdst[:, t * 128:(t + 1) * 128],
                                       pj[0:HD, 0:128], AFT.Identity),
                                   waits=(("pe", t_mm),), inc="s")
                        sl = t % 8

                        def mk_v(e, sl=sl, xbuf=xbuf):
                            r = None
                            for j in range(NE):
                                r = e.matmul(pv[:, sl * 64:(sl + 1) * 64],
                                             xlT[:, xbuf, j],
                                             qkvw[:, 2, j],
                                             start=(j == 0),
                                             stop=(j == NE - 1))
                            return r
                        t_mm = p.emit("tensor", mk_v,
                                      waits=(("s", t_xl), ("s", p.n["s"])),
                                      inc="pe")
                        xl_cons[xbuf] = t_mm
                        p.emit("scalar", lambda e, t=t, sl=sl: e.activation(
                            vv[:, t], pv[:, sl * 64:(sl + 1) * 64],
                            AFT.Identity),
                            waits=(("pe", t_mm),), inc="s")
                    p.barrier("attn_stageA")
                if not stageb:
                    return

                sctx2 = ExitStack()
                with sctx2:
                    poT = sctx2.enter_context(
                        nc.psum_tensor("poT", [64, T], F32))
                    poS = vbuf[0:64, 0:T]
                    scctx = ExitStack()
                    ps = scctx.enter_context(
                        nc.psum_tensor("ps", [128, 4, 512], F32))
                    for kt in range(n_tiles):
                        def mk_sc(e, kt=kt):
                            r = None
                            for c, cw in chunks:
                                r = e.matmul(
                                    ps[:, c, :cw],
                                    kT[:, kt * 128:(kt + 1) * 128],
                                    qT[:, c * 512:c * 512 + cw],
                                    start=True, stop=True)
                            return r
                        t_sc = p.emit("tensor", mk_sc,
                                      waits=(("v", p.n["v"]),
                                             ("s", p.n["s"])), inc="pe")

                        def mk_mx(e):
                            r = None
                            for c, cw in chunks:
                                r = e.tensor_reduce(mx4[:, c:c + 1],
                                                    ps[:, c, :cw],
                                                    AX.X, ALU.max)
                            fil(e)
                            return r
                        p.emit("vector", mk_mx,
                               waits=(("pe", t_sc), ("s", p.n["s"])))
                        t_nm = p.emit("vector", lambda e: e.tensor_reduce(
                            rmax[:], mx4[:, :nic], AX.X, ALU.max,
                            negate=True), inc="v")
                        t_exp = 0
                        pe_prev = p.n["pe"]
                        for c, cw in chunks:
                            t_exp = p.emit(
                                "scalar", lambda e, c=c, cw=cw: e.activation(
                                    attnT[:, c * 512:c * 512 + cw],
                                    ps[:, c, :cw], AFT.Exp, bias=rmax[:]),
                                waits=(("v", t_nm), ("pe", pe_prev)),
                                inc="s")
                        t_z = p.emit("vector", lambda e: e.tensor_reduce(
                            zrow[:], attnT[:], AX.X, ALU.add),
                            waits=(("s", t_exp),), inc="v")
                        p.emit("vector",
                               lambda e: (fil(e),
                                          e.reciprocal(rz[:], zrow[:]))[1])
                        t_vs = p.emit("vector",
                                      lambda e, kt=kt: (fil(e),
                                      e.tensor_scalar_mul(
                                          vsc[:], vv[:, kt], rz[:]))[1],
                                      waits=(("pe", p.n["pe"]),), inc="v")

                        if bmode == "noout":
                            continue

                        def mk_out(e, kt=kt):
                            r = None
                            for c, cw in chunks:
                                r = e.matmul(
                                    poT[:, c * 512:c * 512 + cw],
                                    vsc[:],
                                    attnT[:, c * 512:c * 512 + cw],
                                    start=(kt == 0),
                                    stop=(kt == n_tiles - 1))
                            return r
                        t_om = p.emit("tensor", mk_out,
                                      waits=(("v", t_vs),
                                             ("v", p.n["v"])), inc="pe")
                    scctx.close()
                    if bmode in ("noout", "nopost"):
                        p.barrier("attn_done")
                        return
                    def mk_evo(e):
                        r = None
                        for c, cw in chunks:
                            r = e.activation(poS[:, c * 512:c * 512 + cw],
                                             poT[:, c * 512:c * 512 + cw],
                                             AFT.Identity)
                        return r
                    t_evo = p.emit("scalar", mk_evo,
                                   waits=(("pe", t_om),), inc="s")
                    if bmode == "evoonly":
                        p.barrier("attn_done")
                        return
                    ptb = sctx2.enter_context(
                        nc.psum_tensor("ptb", [128, n_tiles, HD], F32))
                    t_tb = 0
                    for it in range(n_tiles):
                        t_tb = p.emit("tensor", lambda e, it=it: e.transpose(
                            ptb[:, it],
                            poS[:, it * 128:(it + 1) * 128],
                            identf[0:64, 0:64]),
                            waits=(("s", t_evo),), inc="pe")
                    if bmode != "notradd":
                        for it in range(n_tiles):
                            def mk_radd(e, it=it):
                                pb = ptb[:, it].unsqueeze(1).broadcast_to(
                                    (128, 16, HD))
                                xr = x_res[:, it].rearrange(
                                    "p (r c) -> p r c", c=HD)
                                return e.tensor_tensor(xr, xr, pb, ALU.add)
                            p.emit("vector", mk_radd,
                                   waits=(("pe", t_tb),), inc="v")
                    p.barrier("attn_done")

        if stop_after != "w":
            ffn_phase(1)
            if stop_after != "ffn1":
                bmode = "full"
                if stop_after in ("attn_noout", "attn_nopost",
                                  "attn_evoonly", "attn_notradd"):
                    bmode = stop_after.split("_")[1]
                attn_phase(stageb=(stop_after != "attnA"), bmode=bmode)
                if stop_after is None or stop_after == "ffn3":
                    ffn_phase(3)

        t_all = p.n["v"]
        p.emit("gpsimd", lambda e: e.dma_start(
            out=out[:].rearrange("(t p) e -> p t e", p=128), in_=x_res[:]),
            waits=(("v", t_all),), inc="wr", k=16)
        fin = p.n["wr"]
        p.emit("gpsimd", lambda e: None, waits=(("wr", fin),))

        with nc.Block() as block:
            p.run(block)

    return nc



_CACHE = {}


def _get_nc(n_tiles=16, ss=2):
    key = (n_tiles, ss)
    if key not in _CACHE:
        nc = build_program(n_tiles=n_tiles, ss=ss)
        lower_extended_insts(nc)
        _CACHE[key] = nc
    return _CACHE[key]


def kernel(L0_attn_x, L0_attn_Wq, L0_attn_Wk, L0_attn_Wv,
           L0_ffn_w_gate, L0_ffn_w_up, L0_ffn_w_down):
    x = np.ascontiguousarray(np.asarray(L0_attn_x, dtype=np.float32))
    wqT = np.ascontiguousarray(np.asarray(L0_attn_Wq, dtype=np.float32).T)
    wkT = np.ascontiguousarray(np.asarray(L0_attn_Wk, dtype=np.float32).T)
    wvT = np.ascontiguousarray(np.asarray(L0_attn_Wv, dtype=np.float32).T)
    wg = np.ascontiguousarray(np.asarray(L0_ffn_w_gate, dtype=np.float32))
    wu = np.ascontiguousarray(np.asarray(L0_ffn_w_up, dtype=np.float32))
    wd = np.ascontiguousarray(np.asarray(L0_ffn_w_down, dtype=np.float32))
    B = x.shape[0]
    nc = _get_nc()
    in_maps = [{"x": x[b], "wqT": wqT, "wkT": wkT, "wvT": wvT,
                "w_gate": wg, "w_up": wu, "w_down": wd} for b in range(B)]
    res = run_bass_kernel_spmd(nc, in_maps, core_ids=list(range(B)))
    return np.stack([np.asarray(res.results[b]["out"]) for b in range(B)],
                    axis=0).astype(np.float32)


# version 06 (final)
# speedup vs baseline: 1.1798x; 1.0074x over previous
import numpy as np

import concourse.bass as bass
import concourse.mybir as mybir
from concourse.bass_utils import run_bass_kernel_spmd
from concourse.library_overlay import lower_extended_insts
from concourse import dve_ops
from concourse.dve_op import DveOp
from concourse.dve_spec import (
    Spec, Src0, Src1, C0, C1, C2, Zero, One, maxx, minn, lower as dve_lower,
)

F32 = mybir.dt.float32
BF16 = mybir.dt.bfloat16
ALU = mybir.AluOpType
AFT = mybir.ActivationFunctionType
AX = mybir.AxisListType

MAGIC = 12582912.0


def _f(in0):
    return np.float64 if in0.dtype == np.float64 else np.float32


def _ref_v2(in0, in1, s0, s1, imm2):
    p = in0.astype(_f(in0)) * in1
    return (p + p).astype(_f(in0))


def _rne(v, f):
    if f is np.float64:
        return np.round(v)
    m = np.float32(MAGIC)
    return ((v.astype(np.float32) + m) - m).astype(np.float32)


def _ref_qs(in0, in1, s0, s1, imm2):
    f = _f(in0)
    u = np.abs(in0.astype(f))
    return ((u > s0).astype(f) + (u > s1).astype(f)
            + (u > imm2).astype(f) + (u > imm2).astype(f)).astype(f)


def _ref_qt(in0, in1, s0, s1, imm2):
    f = _f(in0)
    v = in0.astype(f)
    a = np.clip(_rne(v, f), -s1, s1)
    return (a * (in1.astype(f) + 2.0)).astype(f)


def _qs_body():
    v = Src0
    u = maxx(v, Zero - v)
    u5 = u > C0
    u7 = u > C1
    u10 = u > C2
    return ((u5 + u7) + u10) + u10


def _qt_body():
    v = Src0
    a = minn(maxx((v + C0) - C0, Zero - C1), C1)
    return a * (Src1 + (One + One))


def _register_ops():
    specs = {
        "ANT_V2": ((lambda p: p + p)(Src0 * Src1), _ref_v2, 28),
        "ANT_QS": (_qs_body(), _ref_qs, 29),
        "ANT_QT": (_qt_body(), _ref_qt, 30),
    }
    out = {}
    have = {o.name: o for o in dve_ops.OPS}
    for name, (body, ref, row) in specs.items():
        if name in have:
            out[name] = have[name]
            continue
        spec = Spec(body=body)
        compiled = dve_lower(spec, ver="v3", perf_en=False, name=name)
        op = DveOp(name, spec, subdim=False,
                   uops_sha={"v3": compiled.sha("v3")}, reference=ref)
        dve_ops.OPS.append(op)
        dve_ops._SUB_OPCODE_FOR_NAME[name] = row
        dve_ops.CUSTOM_DVE_SPECS[name] = spec
        dve_ops.CUSTOM_DVE_REFS[name] = ref
        out[name] = op
    return out


OPS = _register_ops()


ENGINES = ("sync", "gpsimd", "vector", "scalar", "tensor")


class Plan:
    def __init__(self, nc):
        self.nc = nc
        self.q = {e: [] for e in ENGINES}
        self.sem = {}
        self.n = {}

    def mksem(self, ctx, *names):
        for nm in names:
            self.sem[nm] = ctx.enter_context(self.nc.semaphore(nm))
            self.n[nm] = 0

    def emit(self, eng, make, waits=(), inc=None, k=1):
        w = tuple(waits)
        if inc is not None:
            self.n[inc] += k

        def fn(e, make=make, w=w, inc=inc, k=k):
            for s, v in w:
                if v > 0:
                    e.wait_ge(self.sem[s], v)
            r = make(e)
            if inc is not None:
                if r is None:
                    r = e.nop()
                r.then_inc(self.sem[inc], k)
        self.q[eng].append(fn)
        return self.n[inc] if inc is not None else None

    def barrier(self, tag):
        tgt = self.n["ph"] + len(ENGINES)
        fence = self.fence_makers
        self.emit("sync", fence["sync"],
                  waits=tuple((s, self.n[s]) for s in ("lx", "lq", "la", "lb")),
                  inc="ph")
        self.emit("gpsimd", fence["gpsimd"], waits=(("wr", self.n["wr"]),),
                  inc="ph")
        for e in ("vector", "scalar", "tensor"):
            self.emit(e, fence[e], inc="ph")
        for e in ENGINES:
            self.emit(e, lambda en: None, waits=(("ph", tgt),))

    def run(self, block):
        for eng in ENGINES:
            getattr(block, eng)(self._runner(eng))

    def _runner(self, eng):
        fns = self.q[eng]

        def go(e):
            for fn in fns:
                fn(e)
        return go



def build_program(n_tiles=16, ss=2, stop_after=None, taps=()):
    from contextlib import ExitStack

    T = n_tiles * 128
    E, I, HD = 1024, 2816, 64
    NE = E // 128
    NI = I // 128
    IC = [(c * 512, min(512, I - c * 512)) for c in range((I + 511) // 512)]
    NH = E // 128
    assert n_tiles % ss == 0

    nc = bass.Bass("TRN2", target_bir_lowering=False,
                   dynamic_dma_scratch_size=8192,
                   detect_race_conditions=False)

    x_in = nc.dram_tensor("x", [T, E], F32, kind="ExternalInput").ap()
    wq_in = nc.dram_tensor("wqT", [E, HD], F32, kind="ExternalInput").ap()
    wk_in = nc.dram_tensor("wkT", [E, HD], F32, kind="ExternalInput").ap()
    wv_in = nc.dram_tensor("wvT", [E, HD], F32, kind="ExternalInput").ap()
    wg_in = nc.dram_tensor("w_gate", [I, E], F32, kind="ExternalInput").ap()
    wu_in = nc.dram_tensor("w_up", [I, E], F32, kind="ExternalInput").ap()
    wd_in = nc.dram_tensor("w_down", [E, I], F32, kind="ExternalInput").ap()
    out = nc.dram_tensor("out", [T, E], F32, kind="ExternalOutput").ap()
    wd_scr = [nc.dram_tensor(f"wd_scr{j}", [128, NI * 128], BF16,
                             kind="ExternalOutput").ap() for j in range(NH)]

    tap_out = {}
    def mktap(name, shape, dtype):
        tap_out[name] = nc.dram_tensor("tap_" + name, shape, dtype,
                                       kind="ExternalOutput").ap()
        return tap_out[name]

    ctx = ExitStack()
    with ctx:
        p = Plan(nc)
        p.mksem(ctx, "lx", "lq", "la", "lb", "wr", "v", "s", "pe", "g", "ph")

        def fil(e):
            e.tensor_copy(filA[:], filB[:])

        p.fence_makers = {
            "sync": lambda e: e.nop(),
            "gpsimd": lambda e: e.engine_nop(),
            "vector": lambda e: e.tensor_copy(filA[:], filB[:]),
            "scalar": lambda e: e.activation(filA[:], filB[:], AFT.Identity),
            "tensor": lambda e: e.nop(),
        }

        def tap(name, src_ap, after_sem):
            if name not in taps:
                return
            d = mktap(name, src_ap.shape, src_ap.dtype)
            p.emit("gpsimd", lambda e: e.dma_start(out=d[:], in_=src_ap),
                   waits=(after_sem,), inc="wr", k=16)

        def sb(name, shape, dtype):
            return ctx.enter_context(nc.sbuf_tensor(name, shape, dtype))

        ident = sb("ident", [128, 128], BF16)
        identf = sb("identf", [128, 128], F32)
        x_res = sb("x_res", [128, n_tiles, E], F32)
        wgq = sb("wgq", [128, NE, I], BF16)
        wuq = sb("wuq", [128, NE, I], BF16)
        svar = sb("svar", [128, n_tiles], F32)
        ssd = sb("ssd", [128, n_tiles], F32)
        rsig = sb("rsig", [128, n_tiles], F32)
        nbias = sb("nbias", [128, n_tiles], F32)
        smean = sb("smean", [128, n_tiles], F32)

        bnS = sb("bnS", [128, 12], F32)
        filA = sb("filA", [128, 1], F32)
        filB = sb("filB", [128, 1], F32)
        mv1 = sb("mv1", [128, 2], F32)
        mv2 = sb("mv2", [128, 2], F32)
        sm1 = sb("sm1", [128, n_tiles], F32)
        sv1 = sb("sv1", [128, n_tiles], F32)
        sm2 = sb("sm2", [128, n_tiles], F32)
        sv2 = sb("sv2", [128, n_tiles], F32)
        tmpa = sb("tmpa", [128, n_tiles], F32)
        tmpb = sb("tmpb", [128, n_tiles], F32)
        tmpc = sb("tmpc", [128, n_tiles], F32)
        tmpd = sb("tmpd", [128, n_tiles], F32)

        qvh = sb("qvh", [128, I], F32)
        qvx = sb("qvx", [128, E], F32)
        vbuf = sb("vbuf", [128, I], F32)
        qt_o = sb("qt_o", [128, I], BF16)
        qb2 = sb("qb2", [128, I], BF16)
        qb2x = sb("qb2x", [128, E], BF16)
        bmax = sb("bmax", [128, I // 16], F32)
        scl = sb("scl", [128, I // 16], F32)
        rs2 = sb("rs2", [128, I // 16], F32)
        shl = sb("shl", [128, 2, I // 16], F32)

        p.emit("vector", lambda e: e.memset(filB[:], 0.0))
        t_ms = p.emit("gpsimd", lambda e: e.memset(ident[:], 0.0), inc="g")
        p.emit("gpsimd", lambda e: e.affine_select(
            out=ident[:], in_=ident[:],
            compare_op=ALU.not_equal, fill=1.0, base=0,
            pattern=[[-1, 128]], channel_multiplier=1),
            waits=(("g", t_ms),), inc="s")
        t_msf = p.emit("gpsimd", lambda e: e.memset(identf[:], 0.0), inc="g")
        p.emit("gpsimd", lambda e: e.affine_select(
            out=identf[:], in_=identf[:],
            compare_op=ALU.not_equal, fill=1.0, base=0,
            pattern=[[-1, 128]], channel_multiplier=1),
            waits=(("g", t_msf),), inc="s")
        ident_ready = p.n["s"]

        p.emit("sync", lambda e: e.dma_start(
            out=x_res[:], in_=x_in[:].rearrange("(t p) e -> p t e", p=128)),
            inc="lx", k=16)
        x_loaded = p.n["lx"]
        p.tap_fn = tap

        qstate = {
            "par": 0,
            "mulb": [0, 0],
            "mulb_last": 0,
            "free_h": 0,
            "free_x": 0,
        }

        def quant(width, src_ap, waits=(), sh_mult=0.25, amax_fil=False,
                  stream="h"):
            nb = width // 16
            par = qstate["par"]
            qstate["par"] ^= 1
            qdst = qb2x if stream == "x" else qb2
            s3 = src_ap.rearrange("p (b k) -> p b k", k=16)
            p.emit("vector", lambda e: ((fil(e) if amax_fil else None),
                e.tensor_reduce(bmax[:, :nb], s3, AX.X, ALU.max,
                                apply_absolute_value=True))[1], waits=waits)
            p.emit("vector", lambda e: (fil(e), e.tensor_scalar(
                scl[:, :nb], bmax[:, :nb], 1.0 / 6.0, 1e-12,
                ALU.mult, ALU.max))[1])
            p.emit("vector", lambda e: (fil(e), e.reciprocal(
                rs2[:, :nb], scl[:, :nb]))[1])
            p.emit("vector", lambda e, par=par: e.tensor_scalar_mul(
                shl[:, par, :nb], scl[:, :nb], sh_mult),
                waits=(("g", qstate["mulb"][par]),))
            rsb = rs2[:, :nb].unsqueeze(2).broadcast_to((128, nb, 16))
            v3 = vbuf[:, :width].rearrange("p (b k) -> p b k", k=16)
            p.emit("vector", lambda e: e._custom_dve(
                OPS["ANT_V2"], out=v3, in0=s3, in1=rsb))
            p.emit("vector", lambda e: (fil(e), e._custom_dve(
                OPS["ANT_QS"], out=src_ap, in0=vbuf[:, :width],
                s0=5.0, s1=7.0, imm2=10.0))[1])
            t_qt = p.emit("vector", lambda e: (fil(e), e._custom_dve(
                OPS["ANT_QT"], out=qt_o[:, :width], in0=vbuf[:, :width],
                in1=src_ap, s0=MAGIC, s1=4.0))[1],
                waits=(("g", qstate["mulb_last"]),), inc="v")
            shb = shl[:, par, :nb].unsqueeze(2).broadcast_to((128, nb, 16))
            q3 = qdst[:, :width].rearrange("p (b k) -> p b k", k=16)
            t3 = qt_o[:, :width].rearrange("p (b k) -> p b k", k=16)
            t_q = p.emit("gpsimd", lambda e: e.tensor_tensor(
                q3, t3, shb, ALU.mult),
                waits=(("v", t_qt), ("pe", qstate["free_" + stream])),
                inc="g")
            qstate["mulb"][par] = t_q
            qstate["mulb_last"] = t_q
            return t_qt, t_q

        wctx = ExitStack()
        with wctx:
            wld = wctx.enter_context(
                nc.sbuf_tensor("wld", [128, 2, 1408], F32))
            wstg = wctx.enter_context(nc.sbuf_tensor("wstg", [128, 1024], BF16))
            ptrw = wctx.enter_context(nc.psum_tensor("ptrw", [128, 1024], BF16))

            cons_v = [0, 0]

            def wtile(buf, src_dram_ap, width, dst_kind, dst_idx, sh_mult,
                      goff=0):
                lsem = "la" if buf == 0 else "lb"
                t_ld = p.emit("sync", lambda e: e.dma_start(
                    out=wld[:, buf, :width], in_=src_dram_ap),
                    waits=(("v", cons_v[buf]),), inc=lsem, k=16)
                t_qt, t_q = quant(width, wld[:, buf, :width],
                                  waits=((lsem, t_ld),), sh_mult=sh_mult)
                cons_v[buf] = t_qt
                nblk = width // 128
                for g0 in range(0, nblk, 8):
                    gn = min(8, nblk - g0)

                    def mk_tr(e, g0=g0, gn=gn):
                        r = None
                        for j in range(gn):
                            r = e.transpose(
                                ptrw[:, j * 128:(j + 1) * 128],
                                qb2[:, (g0 + j) * 128:(g0 + j + 1) * 128],
                                ident[:])
                        return r
                    t_tr = p.emit("tensor", mk_tr,
                                  waits=(("g", t_q), ("s", ident_ready),
                                         ("wr", p.n["wr"]), ("s", p.n["s"])),
                                  inc="pe")
                    if dst_kind in ("gate", "up"):
                        dst = wgq if dst_kind == "gate" else wuq

                        def mk_ev(e, g0=g0, gn=gn, dst=dst, dst_idx=dst_idx):
                            return e.activation(
                                dst[:, g0:g0 + gn,
                                    dst_idx * 128:(dst_idx + 1) * 128],
                                ptrw[:, :gn * 128]
                                .rearrange("p (c n) -> p c n", n=128),
                                AFT.Identity)
                        p.emit("scalar", mk_ev, waits=(("pe", t_tr),), inc="s")
                    else:
                        def mk_ev(e, gn=gn):
                            return e.activation(wstg[:, :gn * 128],
                                                ptrw[:, :gn * 128],
                                                AFT.Identity)
                        t_ev = p.emit("scalar", mk_ev,
                                      waits=(("pe", t_tr),), inc="s")

                        def mk_st(e, g0=g0 + goff, gn=gn, dst_idx=dst_idx):
                            return e.dma_start(
                                out=wd_scr[dst_idx][:,
                                           g0 * 128:(g0 + gn) * 128],
                                in_=wstg[:, :gn * 128])
                        p.emit("gpsimd", mk_st, waits=(("s", t_ev),),
                               inc="wr", k=16)
                qstate["free_h"] = p.n["pe"]

            wg_t = wg_in.rearrange("(c p) e -> c p e", p=128)
            wu_t = wu_in.rearrange("(c p) e -> c p e", p=128)
            wd_t = wd_in.rearrange("(c p) i -> c p i", p=128)
            bi = 0
            for i in range(NI):
                wtile(bi % 2, wg_t[i], E, "gate", i, 0.25); bi += 1
            for i in range(NI):
                wtile(bi % 2, wu_t[i], E, "up", i, 0.25); bi += 1
            for i in range(NH):
                for h in range(2):
                    wtile(bi % 2, wd_t[i][:, h * 1408:(h + 1) * 1408],
                          1408, "down", i, 0.125, goff=h * 11)
                    bi += 1

            p.barrier("w_done")

        def ln_stats_and_coeffs(first_wait):
            NT = n_tiles
            for t in range(n_tiles):
                def mk_bn(e, t=t):
                    e.bn_stats(bnS[:, 0:6], x_res[:, t, 0:512])
                    e.bn_stats(bnS[:, 6:12], x_res[:, t, 512:1024])
                    e.bn_aggr(mv1[:], bnS[:, 0:6])
                    e.bn_aggr(mv2[:], bnS[:, 6:12])
                    e.tensor_copy(sm1[:, t:t + 1], mv1[:, 0:1])
                    e.tensor_copy(sv1[:, t:t + 1], mv1[:, 1:2])
                    e.tensor_copy(sm2[:, t:t + 1], mv2[:, 0:1])
                    return e.tensor_copy(sv2[:, t:t + 1], mv2[:, 1:2])
                p.emit("vector", mk_bn,
                       waits=(first_wait,) if t == 0 else ())

            def mk_comb(e):
                NTT = n_tiles
                fil(e)
                e.tensor_tensor(smean[:, :NTT], sm1[:, :NTT], sm2[:, :NTT],
                                ALU.add)
                e.tensor_tensor(tmpa[:, :NTT], sm1[:, :NTT], sm1[:, :NTT],
                                ALU.mult)
                e.tensor_tensor(tmpb[:, :NTT], sm2[:, :NTT], sm2[:, :NTT],
                                ALU.mult)
                e.tensor_tensor(tmpc[:, :NTT], sv1[:, :NTT], sv2[:, :NTT],
                                ALU.add)
                e.tensor_scalar_mul(smean[:, :NTT], smean[:, :NTT], 0.5)
                e.tensor_tensor(tmpa[:, :NTT], tmpa[:, :NTT], tmpb[:, :NTT],
                                ALU.add)
                e.tensor_tensor(tmpd[:, :NTT], smean[:, :NTT], smean[:, :NTT],
                                ALU.mult)
                e.tensor_tensor(tmpa[:, :NTT], tmpa[:, :NTT], tmpc[:, :NTT],
                                ALU.add)
                e.tensor_scalar_add(tmpd[:, :NTT], tmpd[:, :NTT], -1e-5)
                fil(e)
                return e.scalar_tensor_tensor(
                    svar[:, :NTT], tmpa[:, :NTT], 0.5, tmpd[:, :NTT],
                    ALU.mult, ALU.subtract)
            t_var = p.emit("vector", mk_comb, inc="v")
            t_sd = p.emit("scalar", lambda e: e.activation(
                ssd[:, :NT], svar[:, :NT], AFT.Sqrt),
                waits=(("v", t_var),), inc="s")
            p.emit("vector", lambda e: e.reciprocal(
                rsig[:, :NT], ssd[:, :NT]), waits=(("s", t_sd),))
            t_ret = p.emit("vector", lambda e: (fil(e),
                e.scalar_tensor_tensor(
                    nbias[:, :NT], smean[:, :NT], -1.0, rsig[:, :NT],
                    ALU.mult, ALU.mult))[1], inc="v")
            return t_ret

        def ffn_phase(tag):
            fctx = ExitStack()
            with fctx:
                xqT = fctx.enter_context(
                    nc.sbuf_tensor(f"xqT{tag}", [128, NE, 128], BF16))
                hqT = fctx.enter_context(
                    nc.sbuf_tensor(f"hqT{tag}", [128, NI, ss * 128], BF16))
                sg = vbuf[:, I - 512:I]
                dnb = fctx.enter_context(
                    nc.sbuf_tensor(f"dnb{tag}", [128, ss * 128], BF16))
                wdq = fctx.enter_context(
                    nc.sbuf_tensor(f"wdq{tag}", [128, 2, NI // 2, 128], BF16))
                pg = fctx.enter_context(
                    nc.psum_tensor(f"pg{tag}", [128, 512], F32))
                pu = fctx.enter_context(
                    nc.psum_tensor(f"pu{tag}", [128, 512], F32))
                ptr = fctx.enter_context(
                    nc.psum_tensor(f"ptf{tag}", [128, 1024], BF16))
                ptr2 = fctx.enter_context(
                    nc.psum_tensor(f"pt2{tag}", [128, ss * 128], BF16))
                pdn = fctx.enter_context(
                    nc.psum_tensor(f"pdn{tag}", [128, ss * 128], F32))

                t_ln = ln_stats_and_coeffs(("lx", x_loaded))

                sts = list(range(n_tiles // ss))
                wd_cons = [0, 0]
                xq_done = {}
                qvx_free = [0]
                hm_gate = [0]

                def emit_xq(t):
                    t_xln = p.emit("scalar", lambda e, t=t: e.activation(
                        qvx[:, :E], x_res[:, t], AFT.Identity,
                        bias=nbias[:, t:t + 1], scale=rsig[:, t:t + 1]),
                        waits=(("v", t_ln), ("v", qvx_free[0])), inc="s")
                    t_qt, t_qx = quant(
                        E, qvx[:, :E], waits=(("s", t_xln),),
                        sh_mult=0.25, stream="x")
                    qvx_free[0] = t_qt
                    xq_done[t] = t_qx

                for st in sts:
                    tiles = list(range(st * ss, (st + 1) * ss))
                    for t in tiles:
                        if t not in xq_done:
                            emit_xq(t)
                        t_qx = xq_done[t]

                        def mk_trx(e):
                            r = None
                            for j in range(NE):
                                r = e.transpose(ptr[:, j * 128:(j + 1) * 128],
                                                qb2x[:, j * 128:(j + 1) * 128],
                                                ident[:])
                            return r
                        t_trx = p.emit("tensor", mk_trx,
                                       waits=(("g", t_qx), ("s", p.n["s"])),
                                       inc="pe")
                        qstate["free_x"] = p.n["pe"]
                        t_evx = p.emit("scalar", lambda e: e.activation(
                            xqT[:].rearrange("p c n -> p (c n)"), ptr[:],
                            AFT.Identity),
                            waits=(("pe", t_trx),), inc="s")

                        for c0, cw in IC:
                            def mk_mm(e, c0=c0, cw=cw):
                                r = None
                                for j in range(NE):
                                    e.matmul(pg[:, :cw], xqT[:, j],
                                             wgq[:, j, c0:c0 + cw],
                                             start=(j == 0),
                                             stop=(j == NE - 1))
                                    r = e.matmul(pu[:, :cw], xqT[:, j],
                                                 wuq[:, j, c0:c0 + cw],
                                                 start=(j == 0),
                                                 stop=(j == NE - 1))
                                return r
                            t_mm = p.emit("tensor", mk_mm,
                                          waits=(("s", t_evx),
                                                 ("v", hm_gate[0]),
                                                 ("s", p.n["s"])), inc="pe")
                            t_silu = p.emit(
                                "scalar", lambda e, cw=cw: e.activation(
                                    sg[:, :cw], pg[:, :cw], AFT.Silu),
                                waits=(("pe", t_mm),), inc="s")
                            hm_gate[0] = p.emit("vector",
                                   lambda e, c0=c0, cw=cw: e.tensor_tensor(
                                       qvh[:, c0:c0 + cw], sg[:, :cw],
                                       pu[:, :cw], ALU.mult),
                                   waits=(("s", t_silu), ("pe", t_mm)),
                                   inc="v")
                        if t + 1 < n_tiles:
                            emit_xq(t + 1)
                        _, t_qh = quant(I, qvh[:, :I], sh_mult=0.25,
                                        amax_fil=True)
                        si = t - st * ss
                        for g0 in range(0, NI, 8):
                            gn = min(8, NI - g0)

                            def mk_trh(e, g0=g0, gn=gn):
                                r = None
                                for j in range(gn):
                                    r = e.transpose(
                                        ptr[:, j * 128:(j + 1) * 128],
                                        qb2[:, (g0 + j) * 128:
                                            (g0 + j + 1) * 128],
                                        ident[:])
                                return r
                            t_trh = p.emit("tensor", mk_trh,
                                           waits=(("g", t_qh),
                                                  ("s", p.n["s"])), inc="pe")

                            def mk_evh(e, g0=g0, gn=gn, si=si):
                                return e.activation(
                                    hqT[:, g0:g0 + gn,
                                        si * 128:(si + 1) * 128],
                                    ptr[:, :gn * 128]
                                    .rearrange("p (c n) -> p c n", n=128),
                                    AFT.Identity)
                            p.emit("scalar", mk_evh,
                                   waits=(("pe", t_trh),), inc="s")
                        qstate["free_h"] = p.n["pe"]

                    NIH = NI // 2
                    add_gate = 0
                    for hs in range(NH):
                        t_dn = 0
                        for half in range(2):
                            buf = (hs * 2 + half) % 2
                            lsem = "la" if buf == 0 else "lb"
                            t_wd = p.emit(
                                "sync",
                                lambda e, hs=hs, half=half, buf=buf:
                                e.dma_start(
                                    out=wdq[:, buf].rearrange(
                                        "p c n -> p (c n)"),
                                    in_=wd_scr[hs][:, half * NIH * 128:
                                                   (half + 1) * NIH * 128]),
                                waits=(("pe", wd_cons[buf]),
                                       ("wr", p.n["wr"])),
                                inc=lsem, k=16)

                            def mk_dn(e, buf=buf, half=half):
                                r = None
                                for jj in range(NIH):
                                    j = half * NIH + jj
                                    r = e.matmul(pdn[:], wdq[:, buf, jj],
                                                 hqT[:, j],
                                                 start=(j == 0),
                                                 stop=(j == NI - 1))
                                return r
                            t_dn = p.emit("tensor", mk_dn,
                                          waits=((lsem, t_wd),
                                                 ("s", p.n["s"])), inc="pe")
                            wd_cons[buf] = t_dn
                        t_evd = p.emit("scalar", lambda e: e.activation(
                            dnb[:], pdn[:], AFT.Identity),
                            waits=(("pe", t_dn),), inc="s")

                        def mk_trb(e):
                            r = None
                            for si in range(ss):
                                r = e.transpose(
                                    ptr2[:, si * 128:(si + 1) * 128],
                                    dnb[:, si * 128:(si + 1) * 128],
                                    ident[:])
                            return r
                        t_trb = p.emit("tensor", mk_trb,
                                       waits=(("s", t_evd),
                                              ("v", add_gate)),
                                       inc="pe")

                        def mk_add(e, hs=hs, st=st):
                            r = None
                            for si in range(ss):
                                t = st * ss + si
                                r = e.tensor_tensor(
                                    x_res[:, t, hs * 128:(hs + 1) * 128],
                                    x_res[:, t, hs * 128:(hs + 1) * 128],
                                    ptr2[:, si * 128:(si + 1) * 128], ALU.add)
                            return r
                        add_gate = p.emit("vector", mk_add,
                                          waits=(("pe", t_trb),), inc="v")

            p.barrier(f"ffn{tag}_done")

        def attn_phase(stageb=True, bmode="full"):
            actx = ExitStack()
            with actx:
                xlT = actx.enter_context(
                    nc.sbuf_tensor("xlT", [128, 2, NE, 128], BF16))
                dump = qb2[:, 0:E]
                vv = qb2x[:, 0:n_tiles * HD].rearrange(
                    "p (t d) -> p t d", d=HD)
                attnT = qt_o[:, 0:T]
                qT = actx.enter_context(nc.sbuf_tensor("qT", [64, T], BF16))
                kT = actx.enter_context(nc.sbuf_tensor("kT", [64, T], BF16))
                vsc = actx.enter_context(
                    nc.sbuf_tensor("vsc", [128, HD], BF16))
                mx4 = actx.enter_context(nc.sbuf_tensor("mx4", [128, 4], F32))
                rmax = actx.enter_context(
                    nc.sbuf_tensor("rmax", [128, 1], F32))
                zrow = actx.enter_context(
                    nc.sbuf_tensor("zrow", [128, 1], F32))
                rz = actx.enter_context(nc.sbuf_tensor("rz", [128, 1], F32))
                qkvw = actx.enter_context(
                    nc.sbuf_tensor("qkvw", [128, 3, NE, HD], BF16))

                wtmp = vbuf[:, 0:NE * HD].rearrange("p (c d) -> p c d", d=HD)
                qkv_ready = 0
                for j, win in enumerate((wq_in, wk_in, wv_in)):
                    t = p.emit("sync", lambda e, win=win: e.dma_start(
                        out=wtmp,
                        in_=win[:].rearrange("(c p) d -> p c d", p=128)),
                        waits=(("v", qkv_ready),), inc="lq", k=16)
                    qkv_ready = p.emit(
                        "vector",
                        lambda e, j=j: e.tensor_copy(qkvw[:, j], wtmp),
                        waits=(("lq", t),), inc="v")

                t_ln = ln_stats_and_coeffs(("v", p.n["v"]))
                chunks = [(c, min(512, T - c * 512))
                          for c in range((T + 511) // 512)]
                nic = len(chunks)

                sctx = ExitStack()
                with sctx:
                    ptra = sctx.enter_context(
                        nc.psum_tensor("ptra", [128, 1024], BF16))
                    pj = sctx.enter_context(
                        nc.psum_tensor("pj", [128, 512], F32))
                    pv = sctx.enter_context(
                        nc.psum_tensor("pv", [128, 512], F32))
                    xl_cons = [0, 0]
                    for t in range(n_tiles):
                        xbuf = t % 2
                        t_x = p.emit("scalar", lambda e, t=t: e.activation(
                            dump[:], x_res[:, t], AFT.Identity,
                            bias=nbias[:, t:t + 1], scale=rsig[:, t:t + 1]),
                            waits=(("v", t_ln), ("pe", p.n["pe"])), inc="s")

                        def mk_tr(e):
                            r = None
                            for j in range(NE):
                                r = e.transpose(
                                    ptra[:, j * 128:(j + 1) * 128],
                                    dump[:, j * 128:(j + 1) * 128], ident[:])
                            return r
                        t_tr = p.emit("tensor", mk_tr,
                                      waits=(("s", t_x), ("s", p.n["s"])),
                                      inc="pe")
                        t_xl = p.emit("scalar",
                                      lambda e, xbuf=xbuf: e.activation(
                                          xlT[:, xbuf],
                                          ptra[:].rearrange(
                                              "p (c n) -> p c n", n=128),
                                          AFT.Identity),
                                      waits=(("pe", t_tr),
                                             ("pe", xl_cons[xbuf])), inc="s")

                        for j2, qkdst in ((0, qT), (1, kT)):
                            def mk_qk(e, j2=j2, xbuf=xbuf):
                                r = None
                                for j in range(NE):
                                    r = e.matmul(
                                        pj[0:HD, 0:128], qkvw[:, j2, j],
                                        xlT[:, xbuf, j],
                                        start=(j == 0), stop=(j == NE - 1))
                                return r
                            t_mm = p.emit("tensor", mk_qk,
                                          waits=(("s", t_xl),
                                                 ("v", qkv_ready),
                                                 ("s", p.n["s"])), inc="pe")
                            p.emit("scalar",
                                   lambda e, qkdst=qkdst, t=t:
                                   e.activation(
                                       qkdst[:, t * 128:(t + 1) * 128],
                                       pj[0:HD, 0:128], AFT.Identity),
                                   waits=(("pe", t_mm),), inc="s")
                        sl = t % 8

                        def mk_v(e, sl=sl, xbuf=xbuf):
                            r = None
                            for j in range(NE):
                                r = e.matmul(pv[:, sl * 64:(sl + 1) * 64],
                                             xlT[:, xbuf, j],
                                             qkvw[:, 2, j],
                                             start=(j == 0),
                                             stop=(j == NE - 1))
                            return r
                        t_mm = p.emit("tensor", mk_v,
                                      waits=(("s", t_xl), ("s", p.n["s"])),
                                      inc="pe")
                        xl_cons[xbuf] = t_mm
                        p.emit("scalar", lambda e, t=t, sl=sl: e.activation(
                            vv[:, t], pv[:, sl * 64:(sl + 1) * 64],
                            AFT.Identity),
                            waits=(("pe", t_mm),), inc="s")
                    p.barrier("attn_stageA")
                if not stageb:
                    return

                sctx2 = ExitStack()
                with sctx2:
                    poT = sctx2.enter_context(
                        nc.psum_tensor("poT", [64, T], F32))
                    poS = vbuf[0:64, 0:T]
                    scctx = ExitStack()
                    ps = scctx.enter_context(
                        nc.psum_tensor("ps", [128, 4, 512], F32))
                    for kt in range(n_tiles):
                        def mk_sc(e, kt=kt):
                            r = None
                            for c, cw in chunks:
                                r = e.matmul(
                                    ps[:, c, :cw],
                                    kT[:, kt * 128:(kt + 1) * 128],
                                    qT[:, c * 512:c * 512 + cw],
                                    start=True, stop=True)
                            return r
                        t_sc = p.emit("tensor", mk_sc,
                                      waits=(("v", p.n["v"]),
                                             ("s", p.n["s"])), inc="pe")

                        def mk_mx(e):
                            r = None
                            for c, cw in chunks:
                                r = e.tensor_reduce(mx4[:, c:c + 1],
                                                    ps[:, c, :cw],
                                                    AX.X, ALU.max)
                            fil(e)
                            return r
                        p.emit("vector", mk_mx,
                               waits=(("pe", t_sc), ("s", p.n["s"])))
                        t_nm = p.emit("vector", lambda e: e.tensor_reduce(
                            rmax[:], mx4[:, :nic], AX.X, ALU.max,
                            negate=True), inc="v")
                        t_exp = 0
                        pe_prev = p.n["pe"]
                        for c, cw in chunks:
                            t_exp = p.emit(
                                "scalar", lambda e, c=c, cw=cw: e.activation(
                                    attnT[:, c * 512:c * 512 + cw],
                                    ps[:, c, :cw], AFT.Exp, bias=rmax[:]),
                                waits=(("v", t_nm), ("pe", pe_prev)),
                                inc="s")
                        t_z = p.emit("vector", lambda e: e.tensor_reduce(
                            zrow[:], attnT[:], AX.X, ALU.add),
                            waits=(("s", t_exp),), inc="v")
                        p.emit("vector",
                               lambda e: (fil(e),
                                          e.reciprocal(rz[:], zrow[:]))[1])
                        t_vs = p.emit("vector",
                                      lambda e, kt=kt: (fil(e),
                                      e.tensor_scalar_mul(
                                          vsc[:], vv[:, kt], rz[:]))[1],
                                      waits=(("pe", p.n["pe"]),), inc="v")

                        if bmode == "noout":
                            continue

                        def mk_out(e, kt=kt):
                            r = None
                            for c, cw in chunks:
                                r = e.matmul(
                                    poT[:, c * 512:c * 512 + cw],
                                    vsc[:],
                                    attnT[:, c * 512:c * 512 + cw],
                                    start=(kt == 0),
                                    stop=(kt == n_tiles - 1))
                            return r
                        t_om = p.emit("tensor", mk_out,
                                      waits=(("v", t_vs),
                                             ("v", p.n["v"])), inc="pe")
                    scctx.close()
                    if bmode in ("noout", "nopost"):
                        p.barrier("attn_done")
                        return
                    def mk_evo(e):
                        r = None
                        for c, cw in chunks:
                            r = e.activation(poS[:, c * 512:c * 512 + cw],
                                             poT[:, c * 512:c * 512 + cw],
                                             AFT.Identity)
                        return r
                    t_evo = p.emit("scalar", mk_evo,
                                   waits=(("pe", t_om),), inc="s")
                    if bmode == "evoonly":
                        p.barrier("attn_done")
                        return
                    ptb = sctx2.enter_context(
                        nc.psum_tensor("ptb", [128, n_tiles, HD], F32))
                    t_tb = 0
                    for it in range(n_tiles):
                        t_tb = p.emit("tensor", lambda e, it=it: e.transpose(
                            ptb[:, it],
                            poS[:, it * 128:(it + 1) * 128],
                            identf[0:64, 0:64]),
                            waits=(("s", t_evo),), inc="pe")
                    if bmode != "notradd":
                        for it in range(n_tiles):
                            def mk_radd(e, it=it):
                                pb = ptb[:, it].unsqueeze(1).broadcast_to(
                                    (128, 16, HD))
                                xr = x_res[:, it].rearrange(
                                    "p (r c) -> p r c", c=HD)
                                return e.tensor_tensor(xr, xr, pb, ALU.add)
                            p.emit("vector", mk_radd,
                                   waits=(("pe", t_tb),), inc="v")
                    p.barrier("attn_done")

        if stop_after != "w":
            ffn_phase(1)
            if stop_after != "ffn1":
                bmode = "full"
                if stop_after in ("attn_noout", "attn_nopost",
                                  "attn_evoonly", "attn_notradd"):
                    bmode = stop_after.split("_")[1]
                attn_phase(stageb=(stop_after != "attnA"), bmode=bmode)
                if stop_after is None or stop_after == "ffn3":
                    ffn_phase(3)

        t_all = p.n["v"]
        p.emit("gpsimd", lambda e: e.dma_start(
            out=out[:].rearrange("(t p) e -> p t e", p=128), in_=x_res[:]),
            waits=(("v", t_all),), inc="wr", k=16)
        fin = p.n["wr"]
        p.emit("gpsimd", lambda e: None, waits=(("wr", fin),))

        with nc.Block() as block:
            p.run(block)

    return nc



_CACHE = {}


def _get_nc(n_tiles=16, ss=2):
    key = (n_tiles, ss)
    if key not in _CACHE:
        nc = build_program(n_tiles=n_tiles, ss=ss)
        lower_extended_insts(nc)
        _CACHE[key] = nc
    return _CACHE[key]


def kernel(L0_attn_x, L0_attn_Wq, L0_attn_Wk, L0_attn_Wv,
           L0_ffn_w_gate, L0_ffn_w_up, L0_ffn_w_down):
    x = np.ascontiguousarray(np.asarray(L0_attn_x, dtype=np.float32))
    wqT = np.ascontiguousarray(np.asarray(L0_attn_Wq, dtype=np.float32).T)
    wkT = np.ascontiguousarray(np.asarray(L0_attn_Wk, dtype=np.float32).T)
    wvT = np.ascontiguousarray(np.asarray(L0_attn_Wv, dtype=np.float32).T)
    wg = np.ascontiguousarray(np.asarray(L0_ffn_w_gate, dtype=np.float32))
    wu = np.ascontiguousarray(np.asarray(L0_ffn_w_up, dtype=np.float32))
    wd = np.ascontiguousarray(np.asarray(L0_ffn_w_down, dtype=np.float32))
    B = x.shape[0]
    nc = _get_nc()
    in_maps = [{"x": x[b], "wqT": wqT, "wkT": wkT, "wvT": wvT,
                "w_gate": wg, "w_up": wu, "w_down": wd} for b in range(B)]
    res = run_bass_kernel_spmd(nc, in_maps, core_ids=list(range(B)))
    return np.stack([np.asarray(res.results[b]["out"]) for b in range(B)],
                    axis=0).astype(np.float32)
